# version 04
# baseline (speedup 1.0000x reference)
import numpy as np
import ml_dtypes
from contextlib import ExitStack
import os as _os

import concourse.bass as bass
import concourse.mybir as mybir
from concourse.bass_utils import run_bass_kernel_spmd

F32 = mybir.dt.float32
BF16 = mybir.dt.bfloat16
F8 = mybir.dt.float8e4
AF = mybir.ActivationFunctionType
ALU = mybir.AluOpType
AX = mybir.AxisListType
BF = ml_dtypes.bfloat16

B, S, D = 2, 4096, 2048
T = 1024
PRE = 16
TL = T + PRE
NH, HD, DST = 64, 64, 128
DIN = NH * HD
DCONV = DIN + 2 * DST
DPROJ = 2 * DIN + 2 * DST + NH
CH = 128
NCH = T // CH
DFF = 8192
EPS = 1e-5
NKD = D // 128
NXBC = DCONV // 128
NZ = DIN // 128
NFF = DFF // 128

DEBUG = ()


class _StopBuild(Exception):
    pass


class Ctr:
    def __init__(self, stack, nc, name):
        self.h = stack.enter_context(nc.semaphore(name))
        self.n = 0

    def inc(self, inst, k=1):
        inst.then_inc(self.h, k)
        self.n += k
        return self.n

    def dinc(self, eng, k=1):
        eng.maybe_drain_then_inc((self.h, k), fusable=False)
        self.n += k
        return self.n


class Arena:
    def __init__(self, nc, name, cols, dtype):
        self.t = nc.alloc_sbuf_tensor(name, [128, cols], dtype)
        self.cols = cols
        self.off = 0
        self.marks = []

    def take(self, cols, rows=128):
        cols = (cols + 15) // 16 * 16
        assert self.off + cols <= self.cols, f"{self.off}+{cols}>{self.cols}"
        ap = self.t[0:rows, self.off:self.off + cols]
        self.off += cols
        return ap

    def push(self):
        self.marks.append(self.off)

    def pop(self):
        self.off = self.marks.pop()


class Builder:
    def __init__(self, debug=()):
        self.debug = debug
        nc = bass.Bass("TRN2", target_bir_lowering=False, detect_race_conditions=False)
        self.nc = nc
        self.stack = ExitStack()

        dram = nc.dram_tensor
        self.i_x = dram("x_slice", [TL, D], F32, kind="ExternalInput")
        self.i_win = dram("w_in_t", [67, 128, 2048], F32, kind="ExternalInput")
        self.i_wout = dram("w_out_t", [16, 128, 4096], F32, kind="ExternalInput")
        self.i_wg0 = dram("w_g0_t", [64, 128, 2048], F32, kind="ExternalInput")
        self.i_wg1 = dram("w_g1_t", [64, 128, 2048], F32, kind="ExternalInput")
        self.i_wlin = dram("w_lin_t", [16, 128, 8192], F32, kind="ExternalInput")
        self.i_convw = dram("conv_w", [DCONV, 4], F32, kind="ExternalInput")
        self.i_convb = dram("conv_b", [DCONV], F32, kind="ExternalInput")
        self.i_A = dram("A_vec", [NH], F32, kind="ExternalInput")
        self.i_D = dram("D_vec", [NH], F32, kind="ExternalInput")
        self.i_dtb = dram("dt_bias", [NH], F32, kind="ExternalInput")
        self.i_normw = dram("norm_w", [DIN], F32, kind="ExternalInput")
        self.i_id_bf = dram("id_bf", [128, 128], BF16, kind="ExternalInput")
        self.i_id_f32 = dram("id_f32", [128, 128], F32, kind="ExternalInput")
        self.i_sel2 = dram("sel2", [2, 128], BF16, kind="ExternalInput")
        self.i_sel2t = dram("sel2t", [64, 32 * 128], BF16, kind="ExternalInput")
        self.i_chainMT = dram("chainMT", [8, 8], F32, kind="ExternalInput")
        self.i_ind = dram("ind_mask", [8], F32, kind="ExternalInput")

        self.o_out = dram("out", [T, D], F32, kind="ExternalOutput")

        _sk = {"kind": "ExternalOutput"} if _os.environ.get("K_SPILLOUT") else {}
        self.s_hln = dram("s_hln", [TL, D], BF16, **_sk)
        self.s_xfm = dram("s_xfm", [DIN, T], BF16, **_sk)
        self.s_z = dram("s_z", [DIN, T], BF16, **_sk)
        self.s_y = dram("s_y", [DIN, T], BF16, **_sk)
        self.s_lc = dram("s_lc", [NCH, 128, DIN], BF16, **_sk)
        self.s_xmid = dram("s_xmid", [T, D], F32, **_sk)
        self.s_h2 = dram("s_h2", [T, D], BF16, **_sk)
        self.s_attnT = dram("s_attnT", [D, T], BF16, **_sk)
        self.s_ffnT = dram("s_ffnT", [D, T], BF16, **_sk)
        self.s_act = dram("s_act", [DFF, T], F8, **_sk)
        self.cc_in = dram("cc_in", [128, DIN + 64], F32)
        self.cc_out = dram("cc_out", [8 * 128, DIN + 64], F32)
        if self.debug:
            self.d_dtx = dram("d_dtx", [128, DIN], BF16)
            self.d_dtxd = dram("d_dtxd", [128, DIN], BF16)
            self.d_decay = dram("d_decay", [128, 64], F32)
            self.d_csT = dram("d_csT", [128, 64], F32)
            self.d_bmt = dram("d_bmt", [128, 128], BF16)
            self.d_g = dram("d_g", [128, 128], BF16)
            self.d_eg = dram("d_eg", [128, 512], BF16)
            self.d_xtok = dram("d_xtok", [128, DIN], BF16)
            self.d_dt = dram("d_dt", [64, T], F32)
            self.d_P = dram("d_P", [128, DIN], F32)
            self.d_prev0 = dram("d_prev0", [128, DIN], BF16)
            self.d_cs = dram("d_cs", [64, T], F32)
            self.d_eg2 = dram("d_eg2", [128, 512], BF16)
            self.d_cshi = dram("d_cshi", [64, T], BF16)
            self.d_cslo = dram("d_cslo", [64, T], BF16)

        A = nc.alloc_sbuf_tensor
        self.id_bf = A("sid_bf", [128, 128], BF16)
        self.id_f32 = A("sid_f32", [128, 128], F32)
        self.sel2 = A("ssel2", [2, 128], BF16)
        self.sel2t = A("ssel2t", [64, 32 * 128], BF16)
        self.ones1f = A("sones1f", [1, 128], F32)
        self.ones1b = A("sones1b", [1, 128], BF16)
        self.onescolb = A("sonescolb", [128, 1], BF16)
        self.convW = A("sconvW", [128, 4 * NXBC], F32)
        self.convB = A("sconvB", [128, NXBC], F32)
        self.acol = A("sacol", [NH, 1], F32)
        self.dtbcol = A("sdtbcol", [NH, 1], F32)
        self.dexp2 = A("sdexp2", [64, 64], F32)
        self.normw = A("snormw", [1, DIN], BF16)
        self.chainMT = A("schainMT", [8, 8], F32)
        self.indcol = A("sindcol", [8, 1], F32)
        self.zerocol = A("szerocol", [64, 1], F32)
        self.epscol = A("sepscol", [128, 1], F32)
        self.dt_sb = A("sdt", [64, T], F32)
        self.cs_sb = A("scs", [64, T], F32)
        self.cshi = A("scshi", [64, T], BF16)
        self.cslo = A("scslo", [64, T], BF16)
        self.expcs = A("sexpcs", [64, T], BF16)
        self.stab = A("sstab", [1, 64 * (NCH + 1)], F32)
        self.cdrow = A("scdrow", [1, 64 * NCH], F32)
        self.expcum = A("sexpcum", [1, 64 * NCH], F32)
        self.coeff = A("scoeff", [8, 64], F32)
        self.coeff0 = A("scoeff0", [8, 64], F32)
        self.smat = A("ssmat", [8, 64], F32)
        self.b_sb = A("sb_sb", [128, T], BF16)
        self.c_sb = A("sc_sb", [128, T], BF16)
        self.g_sb = A("sg_sb", [128, T], BF16)

        self.ar32 = Arena(nc, "ar32", 12544, F32)
        self.ar16 = Arena(nc, "ar16", 55 * 1024, BF16)

        self.pb = [nc.alloc_psum_tensor(f"pb{i}", [128, 512], F32) for i in range(8)]

        st = self.stack
        self.dS = Ctr(st, nc, "dS")
        self.dC = Ctr(st, nc, "dC")
        self.dG = Ctr(st, nc, "dG")
        self.tT = Ctr(st, nc, "tT")
        self.tV = Ctr(st, nc, "tV")
        self.tA = Ctr(st, nc, "tA")
        self.tG = Ctr(st, nc, "tG")
        self.cc = Ctr(st, nc, "ccs")
        self.bar = Ctr(st, nc, "bar")
        self.sA = Ctr(st, nc, "sa")
        self.sM = Ctr(st, nc, "sm")
        self.sY = Ctr(st, nc, "sy")
        self.sS = Ctr(st, nc, "ss")
        self.sZ = Ctr(st, nc, "sz")
        self.sW = Ctr(st, nc, "sw")
        self.sCH = Ctr(st, nc, "sch")

        self.block = None
        self._nbar = 0
        self.dbg = {}

    def sec(self, eng, fn):
        getattr(self.block, eng)(fn)

    def barrier(self):
        self._nbar += 1
        target = self._nbar * 5
        dSn, dCn, dGn = self.dS.n, self.dC.n, self.dG.n
        b = self.bar

        def mk(dh, dn, own_h, own_n):
            def f(eng):
                if dh is not None:
                    eng.wait_ge(dh, dn)
                if own_h is not None:
                    eng.wait_ge(own_h, own_n)
                eng.sem_inc(b.h, 1)
                eng.wait_ge(b.h, target)
            return f
        self.sec("sync", mk(self.dS.h, dSn, None, 0))
        self.sec("scalar", mk(self.dC.h, dCn, self.tA.h, self.tA.n))
        self.sec("gpsimd", mk(self.dG.h, dGn, self.tG.h, self.tG.n))
        self.sec("vector", mk(None, None, self.tV.h, self.tV.n))
        self.sec("tensor", mk(None, None, self.tT.h, self.tT.n))
        b.n = target

    def dump(self, name, ap, shape):
        if name not in self.debug:
            return
        self.barrier()
        o = self.nc.dram_tensor("dbg_" + name, list(shape), ap.dtype, kind="ExternalOutput")
        dG = self.dG

        def f(g):
            dG.inc(g.dma_start(out=o.ap(), in_=ap), 16)
            g.wait_ge(dG.h, dG.n)
        self.sec("gpsimd", f)
        self.barrier()

    def load_w(self, g, src_ap, dst, free_mark):
        if free_mark:
            g.wait_ge(self.tT.h, free_mark)
        return self.dG.inc(g.dma_start(out=dst, in_=src_ap), 16)

    def stage_consts(self):
        dS, dG, tV = self.dS, self.dG, self.tV

        def f_sync(s):
            dS.inc(s.dma_start(out=self.id_bf[:], in_=self.i_id_bf[:, :]), 16)
            dS.inc(s.dma_start(out=self.id_f32[:], in_=self.i_id_f32[:, :]), 16)
            dS.inc(s.dma_start(out=self.sel2[:], in_=self.i_sel2[:, :]), 16)
            dS.inc(s.dma_start(out=self.sel2t[:], in_=self.i_sel2t[:, :]), 16)
            dS.inc(s.dma_start(out=self.convW[:].rearrange("p (m k) -> p m k", k=4),
                               in_=bass.AP(self.i_convw, 0, [[4, 128], [512, NXBC], [1, 4]])), 16)
            dS.inc(s.dma_start(out=self.convB[:],
                               in_=bass.AP(self.i_convb, 0, [[1, 128], [128, NXBC]]),
                               allow_slow_non_contiguous=True), 16)
            dS.inc(s.dma_start(out=self.acol[:, 0], in_=self.i_A[:]), 16)
            dS.inc(s.dma_start(out=self.dtbcol[:, 0], in_=self.i_dtb[:]), 16)
            dS.inc(s.dma_start(out=self.dexp2[:],
                               in_=bass.AP(self.i_D, 0, [[0, 64], [1, 64]])), 16)
            dS.inc(s.dma_start(out=self.chainMT[:], in_=self.i_chainMT[:, :]), 16)
            dS.inc(s.dma_start(out=self.indcol[:, 0], in_=self.i_ind[:]), 16)
        self.sec("sync", f_sync)

        def f_gp(g):
            dG.inc(g.dma_start(out=self.normw[0:1, :],
                               in_=bass.AP(self.i_normw, 0, [[0, 1], [1, DIN]])), 16)
        self.sec("gpsimd", f_gp)

        def f_vec(v):
            v.memset(self.ones1f[:], 1.0)
            v.memset(self.ones1b[:], 1.0)
            v.memset(self.onescolb[:], 1.0)
            v.memset(self.zerocol[:], 0.0)
            v.memset(self.epscol[:], EPS)
            v.memset(self.stab[:, 0:64], 0.0)
            tV.dinc(v)
        self.sec("vector", f_vec)

    def stage_ln(self, src_dram, dst_dram, dstT_sb, rows, with_pre):
        ar32, ar16 = self.ar32, self.ar16
        ar32.push(); ar16.push()
        xt = [ar32.take(D), ar32.take(D)]
        sq = ar32.take(D)
        s1 = ar32.take(16)
        hls = [ar16.take(D), ar16.take(D)]
        dS, dC, tV, tA = self.dS, self.dC, self.tV, self.tA

        tiles = ([(0, PRE)] if with_pre else []) + \
                [((PRE if with_pre else 0) + t * 128, 128)
                 for t in range((rows - (PRE if with_pre else 0)) // 128)]

        m_lastuse = [0, 0]
        m_dma = [0, 0]
        m_red = [0, 0]

        for i, (r0, nr) in enumerate(tiles):
            par = i % 2
            x = xt[par][0:nr, :]
            hl = hls[par][0:nr, :]

            def f_ld(s, r0=r0, nr=nr, x=x, mf=m_lastuse[par]):
                if mf:
                    s.wait_ge(tA.h, mf)
                dS.inc(s.dma_start(out=x, in_=src_dram[r0:r0 + nr, :]), 16)
            self.sec("sync", f_ld)
            m_ld = dS.n

            def f_sq(sc, x=x, nr=nr, m=m_ld, mr=m_red[0]):
                sc.wait_ge(dS.h, m)
                if mr:
                    sc.wait_ge(tV.h, mr)
                sc.activation(out=sq[0:nr, :], in_=x, func=AF.Square)
                tA.dinc(sc)
            self.sec("scalar", f_sq)
            m_sq = tA.n

            def f_v1(v, x=x, nr=nr, m=m_sq, ml=m_ld):
                v.wait_ge(dS.h, ml)
                v.reduce_sum(out=s1[0:nr, 0:1], in_=x, axis=AX.X)
                v.wait_ge(tA.h, m)
                v.reduce_sum(out=s1[0:nr, 1:2], in_=sq[0:nr, :], axis=AX.X)
                tV.dinc(v)
            self.sec("vector", f_v1)
            m_red[0] = tV.n
            m_v1 = tV.n

            def f_s2(sc, nr=nr, m=m_v1):
                sc.wait_ge(tV.h, m)
                sc.activation(out=s1[0:nr, 2:3], in_=s1[0:nr, 0:1], func=AF.Identity,
                              scale=1.0 / D)
                sc.activation(out=s1[0:nr, 7:8], in_=s1[0:nr, 0:1], func=AF.Identity,
                              scale=-1.0 / D)
                sc.activation(out=s1[0:nr, 3:4], in_=s1[0:nr, 2:3], func=AF.Square)
                sc.activation(out=s1[0:nr, 4:5], in_=s1[0:nr, 3:4], func=AF.Identity,
                              scale=-1.0, bias=self.epscol[0:nr, 0:1])
                sc.drain()
                sc.activation(out=s1[0:nr, 5:6], in_=s1[0:nr, 1:2], func=AF.Identity,
                              scale=1.0 / D, bias=s1[0:nr, 4:5])
                tA.dinc(sc)
            self.sec("scalar", f_s2)
            m_s2a = tA.n

            def f_v2(v, nr=nr, m=m_s2a):
                v.wait_ge(tA.h, m)
                v.reciprocal(out=s1[0:nr, 9:10], in_=s1[0:nr, 5:6])
                tV.dinc(v)
            self.sec("vector", f_v2)
            m_v2 = tV.n

            def f_s2b(sc, nr=nr, m=m_v2):
                sc.wait_ge(tV.h, m)
                sc.activation(out=s1[0:nr, 6:7], in_=s1[0:nr, 9:10], func=AF.Sqrt)
                sc.drain()
                sc.activation(out=s1[0:nr, 8:9], in_=s1[0:nr, 7:8], func=AF.Identity,
                              scale=s1[0:nr, 6:7])
                tA.dinc(sc)
            self.sec("scalar", f_s2b)

            def f_s3(sc, x=x, nr=nr, hl=hl, r0=r0, md=m_dma[par]):
                if md:
                    sc.wait_ge(dC.h, md)
                sc.activation(out=hl, in_=x, func=AF.Identity,
                              bias=s1[0:nr, 8:9], scale=s1[0:nr, 6:7])
                tA.dinc(sc)
                dC.inc(sc.dma_start(out=dst_dram[r0:r0 + nr, :], in_=hl), 16)
            self.sec("scalar", f_s3)
            m_lastuse[par] = tA.n
            m_dma[par] = dC.n

            if self.debug and with_pre and i == 0 and "lnpre" in self.debug:
                self.dump("lnpre_s1", s1[0:16, 0:8], [16, 8])
                self.dump("lnpre_hl", hls[0][0:16, :], [16, D])
                self.dump("lnpre_x", xt[0][0:16, :], [16, D])
                self.dump("lnpre_sq", sq[0:16, :], [16, D])

        m_all = self.dC.n

        def f_tr(s):
            s.wait_ge(dC.h, m_all)
            for fc in range(NKD):
                dS.inc(s.dma_start(out=dstT_sb[:, fc * rows:(fc + 1) * rows],
                                   in_=dst_dram[:, fc * 128:(fc + 1) * 128],
                                   transpose=True), 16)
        self.sec("sync", f_tr)
        ar32.pop(); ar16.pop()

    def stage_dt(self):
        ar32 = self.ar32
        ar32.push()
        tmp32 = ar32.take(T, rows=64)
        da32 = ar32.take(T, rows=64)
        dS, dG, tT, tV, tA, sW = self.dS, self.dG, self.tT, self.tV, self.tA, self.sW
        wm = self.wmbuf[0]
        pb = self.pb
        m_hlnT = self.m_hlnT

        def f_gp(g):
            self.load_w(g, self.i_win[66, :, :], wm, 0)
        self.sec("gpsimd", f_gp)
        m_w = dG.n
        self.wfree = [0, 0]

        def f_t(t):
            t.wait_ge(dS.h, m_hlnT)
            t.wait_ge(dG.h, m_w)
            li = None
            for tt in range(2):
                for k in range(NKD):
                    li = t.matmul(out=pb[tt][:, :], lhsT=wm[:, k * 128:(k + 1) * 128],
                                  rhs=self.hlnT[:, k * TL + PRE + tt * 512:k * TL + PRE + (tt + 1) * 512],
                                  start=(k == 0), stop=(k == NKD - 1))
            tT.inc(li)
        self.sec("tensor", f_t)
        self.wfree[0] = tT.n
        m_mm = tT.n

        import os as _os
        _dtcut = _os.environ.get("K_DT", "")

        def f_sc(sc):
            sc.wait_ge(tT.h, m_mm)
            if _dtcut == "mm":
                tA.inc(sc.activation(out=tmp32[:, 0:1], in_=self.dtbcol[:, 0:1], func=AF.Identity))
                return
            for tt in range(2):
                sc.activation(out=tmp32[:, tt * 512:(tt + 1) * 512], in_=pb[tt][0:64, :],
                              func=AF.Exp, bias=self.dtbcol[:, 0:1])
            tA.inc(sc.activation(out=self.dt_sb[:], in_=tmp32[:, :], func=AF.Ln, bias=1.0))
        self.sec("scalar", f_sc)
        self.m_dtact = tA.n

        def f_v(v):
            v.wait_ge(tA.h, self.m_dtact)
            if _dtcut in ("mm", "act"):
                v.memset(self.dt_sb[:], 0.01)
                v.memset(self.cs_sb[:], -0.01)
            else:
                v.tensor_scalar(out=da32[:, :], in0=self.dt_sb[:], scalar1=self.acol[:, 0:1],
                                scalar2=None, op0=ALU.mult)
                if _dtcut == "noscan":
                    v.tensor_copy(out=self.cs_sb[:], in_=da32[:, :])
                else:
                    v.drain()
                    for c in range(NCH):
                        v.tensor_tensor_scan(out=self.cs_sb[0:64, c * CH:(c + 1) * CH],
                                             data0=da32[:, c * CH:(c + 1) * CH],
                                             data1=self.zerocol[:, 0:1].broadcast_to([64, CH]),
                                             initial=0.0, op0=ALU.add, op1=ALU.add)
                    v.drain()
            v.tensor_copy(out=self.cshi[:], in_=self.cs_sb[:])
            v.drain()
            v.tensor_tensor(out=self.cslo[:], in0=self.cs_sb[:], in1=self.cshi[:],
                                   op=ALU.subtract)
            tV.dinc(v)
        self.sec("vector", f_v)
        self.m_cs = tV.n

        def f_sc2(sc):
            sc.wait_ge(tV.h, self.m_cs)
            tA.inc(sc.activation(out=self.expcs[:], in_=self.cs_sb[:], func=AF.Exp))
        self.sec("scalar", f_sc2)
        ar32.pop()

    def stage_xbc(self):
        ar32, ar16 = self.ar32, self.ar16
        ar32.push(); ar16.push()
        raw = [ar32.take(TL + 8), ar32.take(TL + 8)]
        cvt = [ar32.take(T), ar32.take(T)]
        xcs = [ar16.take(T), ar16.take(T)]
        dS, dC, dG, tT, tV, tA, sW = self.dS, self.dC, self.dG, self.tT, self.tV, self.tA, self.sW
        pb = self.pb
        m_hlnT = self.m_hlnT
        m_evac = [0, 0]
        m_silu = [0, 0]
        m_xdma = [0, 0]

        for mi in range(NXBC):
            par = mi % 2
            wm = self.wmbuf[par]
            psC, psA, psB = (pb[0], pb[1], pb[2]) if par == 0 else (pb[3], pb[4], pb[5])

            def f_gp(g, mi=mi, wm=wm, fm=self.wfree[par]):
                self.load_w(g, self.i_win[32 + mi, :, :], wm, fm)
            self.sec("gpsimd", f_gp)
            m_w = dG.n

            def f_t(t, wm=wm, psC=psC, psA=psA, psB=psB, m_w=m_w, me=m_evac[par], first=(mi < 2)):
                t.wait_ge(dS.h, m_hlnT)
                t.wait_ge(dG.h, m_w)
                if first:
                    t.wait_ge(tA.h, self.m_dtact)
                if me:
                    t.wait_ge(tV.h, me)
                li = None
                for ps, c0, w in ((psC, 0, PRE), (psA, PRE, 512), (psB, PRE + 512, 512)):
                    for k in range(NKD):
                        li = t.matmul(out=ps[:, 0:w], lhsT=wm[:, k * 128:(k + 1) * 128],
                                      rhs=self.hlnT[:, k * TL + c0:k * TL + c0 + w],
                                      start=(k == 0), stop=(k == NKD - 1))
                tT.inc(li)
            self.sec("tensor", f_t)
            self.wfree[par] = tT.n
            m_mm = tT.n

            def f_v1(v, r=raw[par], psC=psC, psA=psA, psB=psB, m=m_mm, msil=m_silu[par]):
                v.wait_ge(tT.h, m)
                if msil:
                    v.wait_ge(tA.h, msil)
                v.tensor_copy(out=r[:, 0:16], in_=psC[:, 0:16])
                v.tensor_copy(out=r[:, 16:528], in_=psA[:, :])
                v.tensor_copy(out=r[:, 528:1040], in_=psB[:, :])
                tV.dinc(v)
            self.sec("vector", f_v1)
            m_evac[par] = tV.n
            m_ev = tV.n

            def f_s1(sc, r=raw[par], cv=cvt[par], m=m_ev, mi=mi, msil=m_silu[par]):
                sc.wait_ge(tV.h, m)
                tA.inc(sc.activation(out=cv[:, :], in_=r[:, 13:1037], func=AF.Identity,
                                     scale=self.convW[:, 4 * mi:4 * mi + 1]))
            self.sec("scalar", f_s1)
            m_t0 = tA.n

            def f_v2(v, r=raw[par], cv=cvt[par], m=m_t0, mi=mi):
                v.wait_ge(tA.h, m)
                li = None
                for k in (1, 2, 3):
                    li = v.scalar_tensor_tensor(out=cv[:, :], in0=r[:, 13 + k:1037 + k],
                                                scalar=self.convW[:, 4 * mi + k:4 * mi + k + 1],
                                                in1=cv[:, :], op0=ALU.mult, op1=ALU.add)
                tV.inc(li)
            self.sec("vector", f_v2)
            m_taps = tV.n

            if mi < 32:
                dst = xcs[par]
            elif mi == 32:
                dst = self.b_sb[:, :]
            else:
                dst = self.c_sb[:, :]

            def f_s2(sc, dst=dst, cv=cvt[par], m=m_taps, mi=mi, md=m_xdma[par]):
                sc.wait_ge(tV.h, m)
                if mi < 32 and md:
                    sc.wait_ge(dC.h, md)
                sc.activation(out=dst, in_=cv[:, :], func=AF.Silu,
                              bias=self.convB[:, mi:mi + 1])
                tA.dinc(sc)
                if mi < 32:
                    dC.inc(sc.dma_start(out=self.s_xfm[mi * 128:(mi + 1) * 128, :], in_=dst), 16)
            self.sec("scalar", f_s2)
            m_silu[par] = tA.n
            if mi < 32:
                m_xdma[par] = dC.n
        self.m_bc = self.tA.n
        ar32.pop(); ar16.pop()

    def emit_z_unit(self, zm):
        dS, dC, dG, tT, tV, sW, sZ = self.dS, self.dC, self.dG, self.tT, self.tV, self.sW, self.sZ
        pb2 = self.pb[2]
        par = (35 + zm) % 2
        wm = self.wmbuf[par]

        def f_gp(g, wm=wm, zm=zm, fm=self.wfree[par]):
            self.load_w(g, self.i_win[zm, :, :], wm, fm)
        self.sec("gpsimd", f_gp)
        m_w = dG.n

        for tt in range(2):
            def f_t(t, wm=wm, m_w=m_w, tt=tt, mz=self.sZ.n):
                t.wait_ge(dG.h, m_w)
                if mz:
                    t.wait_ge(sZ.h, mz)
                li = None
                for k in range(NKD):
                    li = t.matmul(out=pb2[:, :], lhsT=wm[:, k * 128:(k + 1) * 128],
                                  rhs=self.hlnT[:, k * TL + PRE + tt * 512:k * TL + PRE + (tt + 1) * 512],
                                  start=(k == 0), stop=(k == NKD - 1))
                tT.inc(li)
            self.sec("tensor", f_t)
            if tt == 1:
                self.wfree[par] = tT.n
            m_mm = tT.n

            def f_v(v, tt=tt, m=m_mm, md=self._z_dma):
                v.wait_ge(tT.h, m)
                if md:
                    v.wait_ge(dC.h, md)
                v.tensor_copy(out=self.zstg[:, tt * 512:(tt + 1) * 512], in_=pb2[:, :])
                sZ.dinc(v)
            self.sec("vector", f_v)

        def f_s(sc, zm=zm, me=self.sZ.n):
            sc.wait_ge(sZ.h, me)
            dC.inc(sc.dma_start(out=self.s_z[zm * 128:(zm + 1) * 128, :], in_=self.zstg[:]), 16)
        self.sec("scalar", f_s)
        self._z_dma = dC.n

    def stage_ssd1(self):
        ar32, ar16 = self.ar32, self.ar16
        ar32.push(); ar16.push()
        csT = ar32.take(64)
        cslast = ar32.take(64, rows=1)
        negcsT = ar32.take(64)
        dtT = ar32.take(64)
        decay = ar32.take(64)
        dtdec = ar32.take(64)
        self.carry = [ar32.take(DIN), ar32.take(DIN)]
        eg32 = [ar32.take(512), ar32.take(512)]
        x_tok = ar16.take(DIN)
        dtx = ar16.take(DIN)
        dtxd = ar16.take(DIN)
        xfmc = ar16.take(2 * DIN, rows=64)
        ystg = ar16.take(2 * DIN, rows=64)
        lcstg = ar16.take(DIN)
        eg = [ar16.take(512), ar16.take(512)]
        self.zstg = ar16.take(T)
        BmT = ar16.take(128)

        dS, dC, dG = self.dS, self.dC, self.dG
        tT, tV, tA, tG = self.tT, self.tV, self.tA, self.tG
        sA, sM, sY, sS, sCH = self.sA, self.sM, self.sY, self.sS, self.sCH
        pb = self.pb
        self._z_dma = 0
        z_next = [0]
        import os as _os
        _cut = _os.environ.get("K_SSD", "")
        _do_quads = _cut not in ("pre", "loads", "tp", "repl", "dtx")
        _do_states = _cut not in ("pre", "quads", "loads", "tp", "repl", "dtx")
        _do_z = _cut not in ("pre", "quads", "states", "loads", "tp", "repl", "dtx")
        _pre_lvl = {"loads": 0, "tp": 1, "repl": 2, "dtx": 3, "g": 4, "cdr": 5, "lc": 6}.get(_cut, 9)
        _do_quads = _do_quads and _pre_lvl >= 9
        _do_states = _do_states and _pre_lvl >= 9
        _do_z = _do_z and _pre_lvl >= 9

        def f_v0(v):
            v.memset(self.carry[0][:, :], 0.0)
            tV.dinc(v)
        self.sec("vector", f_v0)

        def f_g0(g):
            fr = self.stack.enter_context(g.register("fillreg"))
            g.reg_mov(fr, float(-1e30))
            self.fill_val = g.snap(fr)
        self.sec("gpsimd", f_g0)

        m_ystg_dma = 0
        m_lc_dma = 0
        quad_idx = [0]

        for c in range(NCH):
            cur, nxt = self.carry[c % 2], self.carry[(c + 1) % 2]
            csl = slice(c * CH, (c + 1) * CH)

            def f_sy(s, c=c):
                if c >= 1:
                    s.wait_ge(sS.h, c * 8)
                dS.inc(s.dma_start(
                    out=xfmc[:].rearrange("p (h i) -> p h i", h=64),
                    in_=bass.AP(self.s_xfm, c * CH, [[T, 64], [64 * T, 64], [1, CH]])), 16)
                dS.inc(s.dma_start(out=x_tok[:], in_=self.s_xfm[:, c * CH:(c + 1) * CH],
                                   transpose=True), 16)
            self.sec("sync", f_sy)
            m_ld = dS.n

            if _pre_lvl < 1:
                def f_skip(v, cur=cur, nxt=nxt):
                    for sl8 in range(8):
                        v.tensor_copy(out=nxt[:, sl8 * 512:(sl8 + 1) * 512],
                                                  in_=cur[:, sl8 * 512:(sl8 + 1) * 512])
                        self.sS.dinc(v)
                self.sec("vector", f_skip)
                continue

            def f_t1(t, csl=csl, c=c):
                t.wait_ge(tV.h, self.m_cs)
                if c >= 1:
                    t.wait_ge(sS.h, c * 8)
                t.transpose(out=pb[3][0:128, 0:64], in_=self.cs_sb[0:64, csl],
                            identity=self.id_f32[0:64, 0:64])
                t.transpose(out=pb[3][0:128, 64:128], in_=self.dt_sb[0:64, csl],
                            identity=self.id_f32[0:64, 0:64])
                tT.inc(t.transpose(out=pb[3][0:1, 448:512],
                            in_=self.cs_sb[0:64, (c + 1) * CH - 1:(c + 1) * CH],
                            identity=self.id_f32[0:64, 0:64]))
            self.sec("tensor", f_t1)
            m_t1 = tT.n

            def f_v1(v, m=m_t1):
                v.wait_ge(tT.h, m)
                v.tensor_copy(out=csT[:], in_=pb[3][:, 0:64])
                v.tensor_copy(out=dtT[:], in_=pb[3][:, 64:128])
                v.tensor_copy(out=cslast[0:1, :], in_=pb[3][0:1, 448:512])
                v.tensor_scalar(out=negcsT[:], in0=csT[:], scalar1=-1.0,
                                       scalar2=None, op0=ALU.mult)
                tV.dinc(v)
            self.sec("vector", f_v1)
            m_v1 = tV.n

            if _pre_lvl < 2:
                def f_skip(v, cur=cur, nxt=nxt):
                    for sl8 in range(8):
                        v.tensor_copy(out=nxt[:, sl8 * 512:(sl8 + 1) * 512],
                                                  in_=cur[:, sl8 * 512:(sl8 + 1) * 512])
                        self.sS.dinc(v)
                self.sec("vector", f_skip)
                continue

            def f_t2(t, m=m_v1):
                t.wait_ge(tV.h, m)
                tT.inc(t.matmul(out=pb[3][:, 128:192], lhsT=self.ones1f[:], rhs=cslast[0:1, :],
                                start=True, stop=True))
            self.sec("tensor", f_t2)
            m_t2 = tT.n

            def f_v2(v, m=m_t2):
                v.wait_ge(tT.h, m)
                v.tensor_tensor(out=decay[:], in0=pb[3][:, 128:192], in1=csT[:],
                                       op=ALU.subtract)
                tV.dinc(v)
            self.sec("vector", f_v2)
            m_v2 = tV.n

            def f_s1(sc, c=c, m=m_v2):
                sc.wait_ge(tV.h, m)
                sc.activation(out=decay[:], in_=decay[:], func=AF.Exp)
                tA.inc(sc.activation(out=self.cdrow[0:1, c * 64:(c + 1) * 64],
                                     in_=cslast[0:1, :], func=AF.Exp))
            self.sec("scalar", f_s1)
            m_s1 = tA.n

            if _pre_lvl < 3:
                def f_skip(v, cur=cur, nxt=nxt):
                    for sl8 in range(8):
                        v.tensor_copy(out=nxt[:, sl8 * 512:(sl8 + 1) * 512],
                                                  in_=cur[:, sl8 * 512:(sl8 + 1) * 512])
                        self.sS.dinc(v)
                self.sec("vector", f_skip)
                continue

            def f_v3(v, c=c, m=m_s1, mld=m_ld):
                v.wait_ge(tA.h, m)
                v.tensor_tensor(out=self.stab[0:1, (c + 1) * 64:(c + 2) * 64],
                                in0=self.stab[0:1, c * 64:(c + 1) * 64],
                                in1=cslast[0:1, :], op=ALU.add)
                v.tensor_tensor(out=dtdec[:], in0=dtT[:], in1=decay[:], op=ALU.mult)
                v.wait_ge(dS.h, mld)
                v.tensor_tensor(out=dtx[:].rearrange("p (h q) -> p h q", h=64),
                                in0=x_tok[:].rearrange("p (h q) -> p h q", h=64),
                                in1=dtT[:, :, None].broadcast_to([128, 64, 64]), op=ALU.mult)
                v.tensor_tensor(out=dtxd[:].rearrange("p (h q) -> p h q", h=64),
                                       in0=x_tok[:].rearrange("p (h q) -> p h q", h=64),
                                       in1=dtdec[:, :, None].broadcast_to([128, 64, 64]),
                                       op=ALU.mult)
                tV.dinc(v)
            self.sec("vector", f_v3)
            m_dtx = tV.n

            if self.debug and c == 0:
                def f_dbg1(sc, m=m_dtx):
                    sc.wait_ge(tV.h, m)
                    dC.inc(sc.dma_start(out=self.d_dtx.ap(), in_=dtx[:, :]), 16)
                    dC.inc(sc.dma_start(out=self.d_dtxd.ap(), in_=dtxd[:, :]), 16)
                    dC.inc(sc.dma_start(out=self.d_decay.ap(), in_=decay[:, :]), 16)
                    dC.inc(sc.dma_start(out=self.d_csT.ap(), in_=csT[:, :]), 16)
                    dC.inc(sc.dma_start(out=self.d_xtok.ap(), in_=x_tok[:, :]), 16)
                    dC.inc(sc.dma_start(out=self.d_dt.ap(), in_=self.dt_sb[:, :]), 16)
                    dC.inc(sc.dma_start(out=self.d_cs.ap(), in_=self.cs_sb[:, :]), 16)
                self.sec("scalar", f_dbg1)

            if _pre_lvl < 4:
                def f_skip(v, cur=cur, nxt=nxt):
                    for sl8 in range(8):
                        v.tensor_copy(out=nxt[:, sl8 * 512:(sl8 + 1) * 512],
                                                  in_=cur[:, sl8 * 512:(sl8 + 1) * 512])
                        self.sS.dinc(v)
                self.sec("vector", f_skip)
                continue

            def f_syb(s, csl=csl, c=c, m=m_dtx):
                s.wait_ge(tV.h, m)
                dS.inc(s.dma_start(out=BmT[:], in_=self.b_sb[:, csl], transpose=True), 16)
            self.sec("sync", f_syb)
            m_bmt = dS.n

            def f_t3(t, csl=csl, m=m_dtx):
                t.wait_ge(tA.h, self.m_bc)
                t.wait_ge(tV.h, m)
                tT.inc(t.matmul(out=pb[3][:, 192:320], lhsT=self.b_sb[:, csl], rhs=self.c_sb[:, csl],
                                start=True, stop=True))
            self.sec("tensor", f_t3)
            m_t3 = tT.n

            def f_v4(v, c=c, m=m_t3):
                v.wait_ge(tT.h, m)
                v.tensor_copy(out=self.g_sb[:, c * CH:(c + 1) * CH], in_=pb[3][:, 192:320])
                tV.dinc(v)
            self.sec("vector", f_v4)
            m_v4 = tV.n
            if self.debug and c == 0:
                def f_dbg2(sc, m=m_v4):
                    sc.wait_ge(tV.h, m)
                    dC.inc(sc.dma_start(out=self.d_bmt.ap(), in_=BmT[:, :]), 16)
                    dC.inc(sc.dma_start(out=self.d_g.ap(), in_=self.g_sb[:, 0:128]), 16)
                self.sec("scalar", f_dbg2)

            if _pre_lvl < 5:
                def f_skip(v, cur=cur, nxt=nxt):
                    for sl8 in range(8):
                        v.tensor_copy(out=nxt[:, sl8 * 512:(sl8 + 1) * 512],
                                                  in_=cur[:, sl8 * 512:(sl8 + 1) * 512])
                        self.sS.dinc(v)
                self.sec("vector", f_skip)
                continue

            def f_t4(t, c=c, m=m_s1, mv4=m_v4):
                t.wait_ge(tA.h, m)
                t.wait_ge(tV.h, mv4)
                tT.inc(t.matmul(out=pb[3][:, 448:512], lhsT=self.ones1f[:],
                                rhs=self.cdrow[0:1, c * 64:(c + 1) * 64], start=True, stop=True))
            self.sec("tensor", f_t4)
            m_cdrepl = tT.n

            for hq in range(16 if _do_quads else 0):
                qi = quad_idx[0]
                quad_idx[0] += 1
                par = qi % 2
                psA = pb[par]
                egb = eg[par]

                def f_te(t, hq=hq, c=c, psA=psA, qi=qi):
                    if qi >= 2:
                        t.wait_ge(sA.h, qi - 1)
                    li = None
                    for h4 in range(4):
                        h = hq * 4 + h4
                        sel = self.id_bf[0:64, h:h + 1].broadcast_to([64, 128])
                        t.matmul(out=psA[:, h4 * 128:(h4 + 1) * 128], lhsT=sel,
                                 rhs=self.cshi[0:64, c * CH:(c + 1) * CH],
                                 start=True, stop=False)
                        li = t.matmul(out=psA[:, h4 * 128:(h4 + 1) * 128], lhsT=sel,
                                      rhs=self.cslo[0:64, c * CH:(c + 1) * CH],
                                      start=False, stop=True)
                    tT.inc(li)
                self.sec("tensor", f_te)
                m_te = tT.n

                e32 = eg32[par]
                if not hasattr(self, "_m_exp"):
                    self._m_exp = [0, 0]
                    self._m_ty = [0, 0]

                def f_vc(v, psA=psA, e32=e32, m=m_te, mx=self._m_exp[par]):
                    v.wait_ge(tT.h, m)
                    if mx:
                        v.wait_ge(tA.h, mx)
                    v.tensor_copy(out=e32[:, :], in_=psA[:, :])
                    sA.dinc(v)
                self.sec("vector", f_vc)
                m_vc = sA.n

                def f_ge(g, e32=e32, m=m_vc):
                    g.wait_ge(sA.h, m)
                    tG.inc(g.affine_select(e32[:].rearrange("p (h i) -> p h i", h=4),
                                           e32[:].rearrange("p (h i) -> p h i", h=4),
                                           pattern=[[0, 4], [1, 128]],
                                           compare_op=ALU.is_ge, fill=self.fill_val,
                                           base=0, channel_multiplier=-1))
                self.sec("gpsimd", f_ge)
                m_ge = tG.n

                def f_se(sc, hq=hq, e32=e32, egb=egb, m=m_ge, mv=m_v1,
                         mty=self._m_ty[par]):
                    sc.wait_ge(tG.h, m)
                    sc.wait_ge(tV.h, mv)
                    if mty:
                        sc.wait_ge(tT.h, mty)
                    li = None
                    for h4 in range(4):
                        h = hq * 4 + h4
                        li = sc.activation(out=egb[:, h4 * 128:(h4 + 1) * 128],
                                           in_=e32[:, h4 * 128:(h4 + 1) * 128],
                                           func=AF.Exp, bias=negcsT[:, h:h + 1])
                    tA.inc(li)
                self.sec("scalar", f_se)
                self._m_exp[par] = tA.n
                m_se = tA.n

                def f_vm(v, c=c, egb=egb, m=m_se, mg=m_v4):
                    v.wait_ge(tA.h, m)
                    v.wait_ge(tV.h, mg)
                    v.tensor_tensor(
                        out=egb[:].rearrange("p (h i) -> p h i", h=4),
                        in0=egb[:].rearrange("p (h i) -> p h i", h=4),
                        in1=self.g_sb[:, None, c * CH:(c + 1) * CH].broadcast_to([128, 4, CH]),
                        op=ALU.mult)
                    sM.dinc(v)
                self.sec("vector", f_vm)
                m_vm = sM.n
                if self.debug and c == 0 and hq == 0:
                    def f_dbg4(sc, m=m_vm):
                        sc.wait_ge(sM.h, m)
                        dC.inc(sc.dma_start(out=self.d_eg2.ap(), in_=eg[0][:, :]), 16)
                        dC.inc(sc.dma_start(out=self.d_cshi.ap(), in_=self.cshi[:, :]), 16)
                        dC.inc(sc.dma_start(out=self.d_cslo.ap(), in_=self.cslo[:, :]), 16)
                    self.sec("scalar", f_dbg4)
                if self.debug and c == 0 and hq == 0:
                    def f_dbg3(sc, m=m_vm):
                        sc.wait_ge(sM.h, m)
                        dC.inc(sc.dma_start(out=self.d_eg.ap(), in_=eg[0][:, :]), 16)
                    self.sec("scalar", f_dbg3)

                pby = pb[6 + (qi % 2)]

                def f_ty(t, hq=hq, egb=egb, pby=pby, m=m_vm, md=m_dtx, qi=qi):
                    t.wait_ge(sM.h, m)
                    t.wait_ge(tV.h, md)
                    if qi >= 2:
                        t.wait_ge(sY.h, qi - 1)
                    li = None
                    for h4 in range(4):
                        h = hq * 4 + h4
                        li = t.matmul(out=pby[0:64, h4 * 128:(h4 + 1) * 128],
                                      lhsT=dtx[:, h * 64:(h + 1) * 64],
                                      rhs=egb[:, h4 * 128:(h4 + 1) * 128], start=True, stop=True)
                    tT.inc(li)
                self.sec("tensor", f_ty)
                m_ty = tT.n
                self._m_ty[par] = tT.n

                def f_vy(v, hq=hq, pby=pby, m=m_ty, mld=m_ld, myd=m_ystg_dma):
                    v.wait_ge(tT.h, m)
                    v.wait_ge(dS.h, mld)
                    if myd:
                        v.wait_ge(dC.h, myd)
                    for h4 in range(4):
                        h = hq * 4 + h4
                        li = v.scalar_tensor_tensor(
                            out=ystg[0:64, h * 128:(h + 1) * 128],
                            in0=xfmc[0:64, h * 128:(h + 1) * 128],
                            scalar=self.dexp2[:, h:h + 1],
                            in1=pby[0:64, h4 * 128:(h4 + 1) * 128],
                            op0=ALU.mult, op1=ALU.add)
                    sY.inc(li)
                self.sec("vector", f_vy)

                if hq % 4 == 3 and z_next[0] < NZ and _do_z:
                    self.emit_z_unit(z_next[0])
                    z_next[0] += 1

            if _pre_lvl < 6:
                def f_skip(v, cur=cur, nxt=nxt):
                    for sl8 in range(8):
                        v.tensor_copy(out=nxt[:, sl8 * 512:(sl8 + 1) * 512],
                                                  in_=cur[:, sl8 * 512:(sl8 + 1) * 512])
                        self.sS.dinc(v)
                self.sec("vector", f_skip)
                continue

            def f_v5(v, cur=cur, mlc=m_lc_dma):
                if mlc:
                    v.wait_ge(dC.h, mlc)
                v.tensor_copy(out=lcstg[:], in_=cur[:, :])
                tV.dinc(v)
            self.sec("vector", f_v5)
            m_lcst = tV.n

            def f_s2(sc, c=c, m=m_lcst):
                sc.wait_ge(tV.h, m)
                dC.inc(sc.dma_start(out=self.s_lc[c, :, :], in_=lcstg[:]), 16)
            self.sec("scalar", f_s2)
            m_lc_dma = dC.n

            if not _do_states:
                def f_vs0(v, cur=cur, nxt=nxt):
                    sS, tVl = self.sS, self.tV
                    for sl8 in range(8):
                        v.tensor_copy(out=nxt[:, sl8 * 512:(sl8 + 1) * 512],
                                             in_=cur[:, sl8 * 512:(sl8 + 1) * 512])
                        sS.dinc(v)
                self.sec("vector", f_vs0)
                continue
            sS_base = sS.n

            def f_ts(t, c=c, m=m_v4, md=m_dtx, base=sS_base, mb=m_bmt):
                t.wait_ge(tV.h, max(m, md))
                t.wait_ge(dS.h, mb)
                for sl8 in range(8):
                    if base + sl8 - 1 > 0:
                        t.wait_ge(sS.h, base + sl8 - 1)
                    tT.inc(t.matmul(out=pb[4 + sl8 % 2][:, :], lhsT=BmT[:, :],
                                    rhs=dtxd[:, sl8 * 512:(sl8 + 1) * 512], start=True, stop=True))
            self.sec("tensor", f_ts)
            m_ts_base = tT.n - 8

            def f_vs(v, cur=cur, nxt=nxt, mb=m_ts_base, mcd=m_cdrepl, mlc=m_lcst):
                v.wait_ge(tT.h, mcd)
                v.wait_ge(tV.h, mlc)
                for sl8 in range(8):
                    v.wait_ge(tT.h, mb + sl8 + 1)
                    v.tensor_tensor(
                        out=nxt[:, sl8 * 512:(sl8 + 1) * 512].rearrange("p (h q) -> p h q", h=8),
                        in0=cur[:, sl8 * 512:(sl8 + 1) * 512].rearrange("p (h q) -> p h q", h=8),
                        in1=pb[3][:, 448 + sl8 * 8:456 + sl8 * 8][:, :, None].broadcast_to([128, 8, 64]),
                        op=ALU.mult)
                    v.tensor_tensor(out=nxt[:, sl8 * 512:(sl8 + 1) * 512],
                                           in0=nxt[:, sl8 * 512:(sl8 + 1) * 512],
                                           in1=pb[4 + sl8 % 2][:, :], op=ALU.add)
                    sS.dinc(v)
            self.sec("vector", f_vs)

            if not _do_quads:
                continue

            def f_s3(sc, c=c, m=sS.n):
                sc.wait_ge(sS.h, m)
                dC.inc(sc.dma_start(
                    out=bass.AP(self.s_y, c * CH, [[T, 64], [64 * T, 64], [1, CH]]),
                    in_=ystg[:].rearrange("p (h i) -> p h i", h=64)), 16)
            self.sec("scalar", f_s3)
            m_ystg_dma = dC.n

        while z_next[0] < NZ and _do_z:
            self.emit_z_unit(z_next[0])
            z_next[0] += 1
        self.final_carry = self.carry[NCH % 2]
        ar16.pop()
        self._ssd32_open = True

    def stage_ag(self):
        dG, cc, tV = self.dG, self.cc, self.tV
        fc = self.final_carry
        spad = self.ar32.take(64)

        def f_vz(v):
            v.memset(spad[:, :], 0.0)
            v.drain()
            v.tensor_copy(out=spad[0:1, :], in_=self.stab[0:1, NCH * 64:(NCH + 1) * 64])
            tV.dinc(v)
        self.sec("vector", f_vz)
        m_sp = tV.n

        def f_gp(g):
            g.wait_ge(tV.h, m_sp)
            dG.inc(g.dma_start(out=self.cc_in[:, 0:DIN], in_=fc[:, :]), 16)
            dG.inc(g.dma_start(out=self.cc_in[:, DIN:DIN + 64], in_=spad[:, :]), 16)
            g.wait_ge(dG.h, dG.n)
            cc.inc(g.collective_compute(
                "AllGather", ALU.bypass, replica_groups=[list(range(8))],
                ins=[self.cc_in.ap().opt()], outs=[self.cc_out.ap().opt()]), 1)
            g.wait_ge(cc.h, cc.n)
        self.sec("gpsimd", f_gp)

    def stage_pcomb(self):
        ar32 = self.ar32
        if self._ssd32_open:
            self.ar32.pop()
            self._ssd32_open = False
        ar32.push()
        P = ar32.take(DIN)
        aq = ar32.take(DIN)
        t32 = ar32.take(DIN)
        self.P = P
        dS, tT, tV, tA = self.dS, self.tT, self.tV, self.tA
        pb = self.pb
        RL = DIN + 64

        def f_sy(s):
            dS.inc(s.dma_start(out=self.smat[:, :],
                               in_=bass.AP(self.cc_out, DIN, [[128 * RL, 8], [1, 64]])), 16)
        self.sec("sync", f_sy)
        m_sm = dS.n

        def f_t(t):
            t.wait_ge(dS.h, m_sm)
            tT.inc(t.matmul(out=pb[0][0:8, 0:64], lhsT=self.chainMT[:, :], rhs=self.smat[:, :],
                            start=True, stop=True))
        self.sec("tensor", f_t)
        m_mm = tT.n

        def f_sc(sc):
            sc.wait_ge(tT.h, m_mm)
            sc.activation(out=self.coeff0[:, :], in_=pb[0][0:8, 0:64], func=AF.Exp)
            tA.inc(sc.activation(out=self.expcum[0:1, :], in_=self.stab[0:1, 0:NCH * 64],
                                 func=AF.Exp))
        self.sec("scalar", f_sc)
        m_co = tA.n

        def f_v0(v):
            v.wait_ge(tA.h, m_co)
            v.tensor_scalar(out=self.coeff[:, :], in0=self.coeff0[:, :],
                            scalar1=self.indcol[:, 0:1], scalar2=None, op0=ALU.mult)
            v.memset(P[:, :], 0.0)
            tV.dinc(v)
        self.sec("vector", f_v0)
        m_v0 = tV.n

        for qp in range(8):
            def f_sy2(s, qp=qp, mu=tV.n):
                s.wait_ge(tV.h, mu)
                dS.inc(s.dma_start(out=aq[:, :], in_=self.cc_out[qp * 128:(qp + 1) * 128, 0:DIN]), 16)
            self.sec("sync", f_sy2)
            m_aq = dS.n

            def f_t2(t, qp=qp, mv=max(m_v0, tV.n)):
                t.wait_ge(tV.h, mv)
                tT.inc(t.matmul(out=pb[1][:, 0:64],
                                lhsT=self.id_f32[0:8, qp:qp + 1].broadcast_to([8, 128]),
                                rhs=self.coeff[0:8, 0:64], start=True, stop=True))
            self.sec("tensor", f_t2)
            m_rep = tT.n

            def f_v1(v, m=m_aq, mr=m_rep):
                v.wait_ge(dS.h, m)
                v.wait_ge(tT.h, mr)
                v.tensor_tensor(out=t32[:].rearrange("p (h q) -> p h q", h=64),
                                in0=aq[:].rearrange("p (h q) -> p h q", h=64),
                                in1=pb[1][:, 0:64][:, :, None].broadcast_to([128, 64, 64]),
                                op=ALU.mult)
                v.tensor_tensor(out=P[:, :], in0=P[:, :], in1=t32[:, :], op=ALU.add)
                tV.dinc(v)
            self.sec("vector", f_v1)
        if self.debug:
            def f_dbgP(sc, m=tV.n):
                sc.wait_ge(tV.h, m)
                self.dC.inc(sc.dma_start(out=self.d_P.ap(), in_=P[:, :]), 16)
            self.sec("scalar", f_dbgP)
        ar32.pop()
        ar32.push()
        self.P = ar32.take(DIN)

    def stage_gate_wout(self):
        ar32, ar16 = self.ar32, self.ar16
        ar32.push(); ar16.push()
        t32 = ar32.take(DIN)
        prevs = [ar16.take(DIN) for _ in range(4)]
        lcr = [ar16.take(DIN), ar16.take(DIN)]
        yr = [ar16.take(512), ar16.take(512)]
        zr = [ar16.take(512), ar16.take(512)]
        e2 = ar16.take(512)
        tb = ar16.take(512)
        yu = ar16.take(512)
        szt = ar16.take(512)
        sqs = ar16.take(512)
        ygsb = ar16.take(32 * 512)
        rmsrow = ar16.take(512)
        rms32 = ar32.take(1024, rows=1)
        wom = ar16.take(DIN)
        att = [ar16.take(512), ar16.take(512)]
        dS, dC, dG = self.dS, self.dC, self.dG
        tT, tV, tA = self.tT, self.tV, self.tA
        sW = self.sW
        pb = self.pb
        P = self.P

        wo_free = [0]
        for sb in range(2):
            for ci in range(4):
                c = sb * 4 + ci

                def f_sy(s, c=c, ci=ci, mu=tV.n):
                    if mu:
                        s.wait_ge(tV.h, mu)
                    dS.inc(s.dma_start(out=lcr[ci % 2][:, :], in_=self.s_lc[c, :, :]), 16)
                self.sec("sync", f_sy)
                m_lc = dS.n

                def f_t(t, c=c, mprev=tV.n):
                    if mprev:
                        t.wait_ge(tV.h, mprev)
                    tT.inc(t.matmul(out=pb[2][:, 0:64], lhsT=self.ones1f[:],
                                    rhs=self.expcum[0:1, c * 64:(c + 1) * 64], start=True, stop=True))
                self.sec("tensor", f_t)
                m_rep = tT.n

                def f_v(v, ci=ci, m=m_lc, mr=m_rep):
                    v.wait_ge(tT.h, mr)
                    v.tensor_tensor(out=t32[:].rearrange("p (h q) -> p h q", h=64),
                                    in0=P[:].rearrange("p (h q) -> p h q", h=64),
                                    in1=pb[2][:, 0:64][:, :, None].broadcast_to([128, 64, 64]),
                                    op=ALU.mult)
                    v.wait_ge(dS.h, m)
                    v.tensor_tensor(out=prevs[ci][:, :], in0=t32[:, :],
                                           in1=lcr[ci % 2][:, :], op=ALU.add)
                    tV.dinc(v)
                self.sec("vector", f_v)
            m_prevs = tV.n
            if self.debug and sb == 0:
                def f_dbgpr(sc, m=m_prevs):
                    sc.wait_ge(tV.h, m)
                    self.dC.inc(sc.dma_start(out=self.d_prev0.ap(), in_=prevs[0][:, :]), 16)
                self.sec("scalar", f_dbgpr)

            for g in range(32):
                def f_sy2(s, g=g, sb=sb, mu=tV.n if g >= 2 else 0):
                    if mu:
                        s.wait_ge(tV.h, mu)
                    dS.inc(s.dma_start(out=yr[g % 2][:, :],
                                       in_=bass.AP(self.s_y, g * 128 * T + sb * 512,
                                                   [[T, 128], [1, 512]])), 16)
                    dS.inc(s.dma_start(out=zr[g % 2][:, :],
                                       in_=bass.AP(self.s_z, g * 128 * T + sb * 512,
                                                   [[T, 128], [1, 512]])), 16)
                self.sec("sync", f_sy2)
                m_yz = dS.n

                def f_t2(t, g=g, sb=sb, mp=m_prevs, me=tV.n):
                    t.wait_ge(tV.h, mp)
                    if me:
                        t.wait_ge(tV.h, me)
                    for ci in range(4):
                        c = sb * 4 + ci
                        t.matmul(out=pb[3][:, ci * 128:(ci + 1) * 128],
                                 lhsT=prevs[ci][:, g * 128:(g + 1) * 128],
                                 rhs=self.c_sb[:, c * CH:(c + 1) * CH], start=True, stop=True)
                    tT.inc(t.matmul(out=pb[4][:, :],
                                    lhsT=self.sel2t[:, g * 128:(g + 1) * 128],
                                    rhs=self.expcs[0:64, sb * 512:(sb + 1) * 512],
                                    start=True, stop=True))
                self.sec("tensor", f_t2)
                m_t2 = tT.n

                def f_v2(v, g=g, m=m_t2, myz=m_yz):
                    v.wait_ge(tT.h, m)
                    v.tensor_copy(out=e2[:, :], in_=pb[4][:, :])
                    v.drain()
                    v.tensor_tensor(out=tb[:, :], in0=pb[3][:, :], in1=e2[:, :], op=ALU.mult)
                    v.wait_ge(dS.h, myz)
                    v.tensor_tensor(out=yu[:, :], in0=yr[g % 2][:, :], in1=tb[:, :],
                                           op=ALU.add)
                    tV.dinc(v)
                self.sec("vector", f_v2)
                m_v2 = tV.n

                def f_sc2(sc, g=g, m=m_yz, mv=m_v2):
                    sc.wait_ge(dS.h, m)
                    tA.inc(sc.activation(out=szt[:, :], in_=zr[g % 2][:, :], func=AF.Silu))
                self.sec("scalar", f_sc2)
                m_sz = tA.n

                def f_v3(v, g=g, m=m_sz):
                    v.wait_ge(tA.h, m)
                    v.tensor_tensor(out=ygsb[:, g * 512:(g + 1) * 512],
                                           in0=yu[:, :], in1=szt[:, :], op=ALU.mult)
                    tV.dinc(v)
                self.sec("vector", f_v3)
                m_yg = tV.n

                def f_sc3(sc, g=g, m=m_yg):
                    sc.wait_ge(tV.h, m)
                    tA.inc(sc.activation(out=sqs[:, :], in_=ygsb[:, g * 512:(g + 1) * 512],
                                         func=AF.Square))
                self.sec("scalar", f_sc3)
                m_sq = tA.n

                def f_t3(t, g=g, m=m_sq):
                    t.wait_ge(tA.h, m)
                    tT.inc(t.matmul(out=pb[5][0:1, :], lhsT=self.onescolb[:, :], rhs=sqs[:, :],
                                    start=(g == 0), stop=(g == 31)))
                self.sec("tensor", f_t3)
            m_ss = self.tT.n

            def f_v4b(v, m=m_ss):
                v.wait_ge(tT.h, m)
                v.tensor_scalar(out=rms32[0:1, 0:512], in0=pb[5][0:1, :], scalar1=1.0 / DIN,
                                scalar2=EPS, op0=ALU.mult, op1=ALU.add)
                v.drain()
                v.reciprocal(out=rms32[0:1, 512:1024], in_=rms32[0:1, 0:512])
                tV.dinc(v)
            self.sec("vector", f_v4b)
            m_rv = tV.n

            def f_sc4(sc, m=m_rv):
                sc.wait_ge(tV.h, m)
                tA.inc(sc.activation(out=rmsrow[0:1, 0:512], in_=rms32[0:1, 512:1024],
                                     func=AF.Sqrt))
            self.sec("scalar", f_sc4)
            m_rms = tA.n

            for g in range(32):
                def f_t4(t, g=g, m=m_rms, me=tV.n):
                    t.wait_ge(tA.h, m)
                    if g >= 2:
                        t.wait_ge(tV.h, me)
                    tT.inc(t.matmul(out=pb[6 + g % 2][:, :], lhsT=self.normw[0:1, g * 128:(g + 1) * 128],
                                    rhs=rmsrow[0:1, 0:512], start=True, stop=True))
                self.sec("tensor", f_t4)
                m_t4 = tT.n

                def f_v5(v, g=g, m=m_t4):
                    v.wait_ge(tT.h, m)
                    v.tensor_tensor(out=ygsb[:, g * 512:(g + 1) * 512],
                                           in0=ygsb[:, g * 512:(g + 1) * 512],
                                           in1=pb[6 + g % 2][:, :], op=ALU.mult)
                    tV.dinc(v)
                self.sec("vector", f_v5)
            m_yn = tV.n

            for d in range(16):
                def f_gp(g_, d=d, fm=wo_free[0]):
                    if fm:
                        g_.wait_ge(tT.h, fm)
                    self.dG.inc(g_.dma_start(out=wom[:, :], in_=self.i_wout[d, :, :]), 16)
                self.sec("gpsimd", f_gp)
                m_w = dG.n

                def f_t5(t, d=d, m=m_yn, mw=m_w, me=tV.n):
                    t.wait_ge(tV.h, m)
                    t.wait_ge(dG.h, mw)
                    if me:
                        t.wait_ge(tV.h, me)
                    li = None
                    for k in range(32):
                        li = t.matmul(out=pb[0][:, :], lhsT=wom[:, k * 128:(k + 1) * 128],
                                      rhs=ygsb[:, k * 512:(k + 1) * 512],
                                      start=(k == 0), stop=(k == 31))
                    tT.inc(li)
                self.sec("tensor", f_t5)
                wo_free[0] = tT.n
                m_t5 = tT.n

                def f_v6(v, d=d, m=m_t5, md=dC.n):
                    v.wait_ge(tT.h, m)
                    if md:
                        v.wait_ge(dC.h, md)
                    v.tensor_copy(out=att[d % 2][:, :], in_=pb[0][:, :])
                    tV.dinc(v)
                self.sec("vector", f_v6)
                m_att = tV.n

                def f_sc5(sc, d=d, sb=sb, m=m_att):
                    sc.wait_ge(tV.h, m)
                    dC.inc(sc.dma_start(
                        out=bass.AP(self.s_attnT, d * 128 * T + sb * 512, [[T, 128], [1, 512]]),
                        in_=att[d % 2][:, :]), 16)
                self.sec("scalar", f_sc5)
        ar32.pop(); ar16.pop()

    def stage_resid1(self):
        ar32, ar16 = self.ar32, self.ar16
        ar32.push(); ar16.push()
        xm = ar32.take(D)
        xt = ar32.take(D)
        at = ar16.take(D)
        dS, dC, tV = self.dS, self.dC, self.tV
        for tt in range(8):
            def f_sy(s, tt=tt, mu=tV.n):
                if mu:
                    s.wait_ge(tV.h, mu)
                dS.inc(s.dma_start(out=at[:, :], in_=self.s_attnT[:, tt * 128:(tt + 1) * 128],
                                   transpose=True), 16)
                dS.inc(s.dma_start(out=xt[:, :], in_=self.i_x[PRE + tt * 128:PRE + (tt + 1) * 128, :]), 16)
            self.sec("sync", f_sy)
            m_ld = dS.n

            def f_v(v, m=m_ld, md=dC.n):
                v.wait_ge(dS.h, m)
                if md:
                    v.wait_ge(dC.h, md)
                v.tensor_tensor(out=xm[:, :], in0=xt[:, :], in1=at[:, :], op=ALU.add)
                tV.dinc(v)
            self.sec("vector", f_v)
            m_xm = tV.n

            def f_sc(sc, tt=tt, m=m_xm):
                sc.wait_ge(tV.h, m)
                dC.inc(sc.dma_start(out=self.s_xmid[tt * 128:(tt + 1) * 128, :], in_=xm[:, :]), 16)
            self.sec("scalar", f_sc)
        ar32.pop(); ar16.pop()

    def stage_ffn(self):
        ar16 = self.ar16
        dS, dC, dG = self.dS, self.dC, self.dG
        tT, tV, tA, sW = self.tT, self.tV, self.tA, self.sW
        pb = self.pb

        ar16.push()
        wb0 = [ar16.take(1024)[:, :].bitcast(F8) for _ in range(2)]
        wb1 = [ar16.take(1024)[:, :].bitcast(F8) for _ in range(2)]
        gto = ar16.take(T)
        acto = ar16.take(T)
        h2T8 = ar16.take(NKD * T // 2)[:, :].bitcast(F8)
        tV, tA = self.tV, self.tA

        def f_vc8(v):
            v.tensor_copy(out=h2T8, in_=self.h2T[:, :])
            tV.dinc(v)
        self.sec("vector", f_vc8)
        m_h28 = tV.n
        wg_free = [0, 0]
        for o in range(NFF):
            par = o % 2

            def f_gp(g, o=o, par=par, fm=wg_free[par]):
                if fm:
                    g.wait_ge(tT.h, fm)
                self.dG.inc(g.dma_start(out=wb0[par][:, :], in_=self.i_wg0[o, :, :]), 16)
                self.dG.inc(g.dma_start(out=wb1[par][:, :], in_=self.i_wg1[o, :, :]), 16)
            self.sec("gpsimd", f_gp)
            m_w = dG.n

            def f_t(t, o=o, par=par, mw=m_w, ma=tA.n, mv=tV.n):
                t.wait_ge(dG.h, mw)
                t.wait_ge(tV.h, m_h28)
                if ma:
                    t.wait_ge(tA.h, ma)
                if mv:
                    t.wait_ge(tV.h, mv)
                for blk in range(2):
                    for k in range(NKD):
                        t.matmul(out=pb[blk][:, :], lhsT=wb0[par][:, k * 128:(k + 1) * 128],
                                 rhs=h2T8[:, k * T + blk * 512:k * T + blk * 512 + 512],
                                 start=(k == 0), stop=(k == NKD - 1))
                li = None
                for blk in range(2):
                    for k in range(NKD):
                        li = t.matmul(out=pb[2 + blk][:, :], lhsT=wb1[par][:, k * 128:(k + 1) * 128],
                                      rhs=h2T8[:, k * T + blk * 512:k * T + blk * 512 + 512],
                                      start=(k == 0), stop=(k == NKD - 1))
                tT.inc(li)
            self.sec("tensor", f_t)
            wg_free[par] = tT.n
            m_mm = tT.n

            def f_sc(sc, o=o, m=m_mm):
                sc.wait_ge(tT.h, m)
                li = None
                for blk in range(2):
                    li = sc.activation(out=gto[:, blk * 512:(blk + 1) * 512], in_=pb[blk][:, :],
                                       func=AF.Gelu_apprx_tanh)
                tA.inc(li)
            self.sec("scalar", f_sc)
            m_gelu = tA.n

            def f_v(v, o=o, m=m_gelu, mm=m_mm, md=dG.n):
                v.wait_ge(tA.h, m)
                v.wait_ge(tT.h, mm)
                if md:
                    v.wait_ge(dG.h, md)
                li = None
                for blk in range(2):
                    li = v.tensor_tensor(out=acto[:, blk * 512:(blk + 1) * 512],
                                         in0=gto[:, blk * 512:(blk + 1) * 512],
                                         in1=pb[2 + blk][:, :], op=ALU.mult)
                tV.inc(li)
            self.sec("vector", f_v)
            m_act = tV.n

            def f_s2(g, o=o, m=m_act):
                g.wait_ge(tV.h, m)
                dG.inc(g.dma_start(out=self.s_act[o * 128:(o + 1) * 128, :], in_=acto[:, :]), 16)
            self.sec("gpsimd", f_s2)
        ar16.pop()
        self._end_j1()

        ar16.push()
        actsb = ar16.take(NFF * 512 // 2)[:, :].bitcast(F8)
        wlb = [ar16.take(DFF // 2)[:, :].bitcast(F8) for _ in range(2)]
        fst = [ar16.take(512), ar16.take(512)]
        wl_free = [0, 0]
        for blk in range(2):
            def f_sy(s, blk=blk, mu=tV.n):
                if mu:
                    s.wait_ge(tV.h, mu)
                dS.inc(s.dma_start(
                    out=actsb.rearrange("p (f i) -> p f i", f=NFF),
                    in_=bass.AP(self.s_act, blk * 512, [[T, 128], [128 * T, NFF], [1, 512]])), 16)
            self.sec("sync", f_sy)
            m_act2 = dS.n

            for d in range(16):
                par = d % 2

                def f_gp(g, d=d, par=par, fm=wl_free[par]):
                    if fm:
                        g.wait_ge(tT.h, fm)
                    self.dG.inc(g.dma_start(out=wlb[par][:, :], in_=self.i_wlin[d, :, :]), 16)
                self.sec("gpsimd", f_gp)
                m_w = dG.n

                def f_t(t, d=d, par=par, mw=m_w, ma=m_act2, me=tV.n):
                    t.wait_ge(dG.h, mw)
                    t.wait_ge(dS.h, ma)
                    if me:
                        t.wait_ge(tV.h, me)
                    li = None
                    for k in range(NFF):
                        li = t.matmul(out=pb[4 + d % 2][:, :], lhsT=wlb[par][:, k * 128:(k + 1) * 128],
                                      rhs=actsb[:, k * 512:(k + 1) * 512],
                                      start=(k == 0), stop=(k == NFF - 1))
                    tT.inc(li)
                self.sec("tensor", f_t)
                wl_free[par] = tT.n
                m_mm = tT.n

                def f_v(v, d=d, m=m_mm, md=dC.n):
                    v.wait_ge(tT.h, m)
                    if md:
                        v.wait_ge(dC.h, md)
                    v.tensor_copy(out=fst[d % 2][:, :], in_=pb[4 + d % 2][:, :])
                    tV.dinc(v)
                self.sec("vector", f_v)
                m_f = tV.n

                def f_sc(sc, d=d, blk=blk, m=m_f):
                    sc.wait_ge(tV.h, m)
                    dC.inc(sc.dma_start(
                        out=bass.AP(self.s_ffnT, d * 128 * T + blk * 512, [[T, 128], [1, 512]]),
                        in_=fst[d % 2][:, :]), 16)
                self.sec("scalar", f_sc)
        ar16.pop()

    def _end_j1(self):
        self.barrier()
        self.ar16.pop()
        self._h2t_popped = True

    def stage_final(self):
        ar32, ar16 = self.ar32, self.ar16
        ar32.push(); ar16.push()
        xm = ar32.take(D)
        ot = ar32.take(D)
        ft = ar16.take(D)
        dS, dC, tV = self.dS, self.dC, self.tV
        for tt in range(8):
            def f_sy(s, tt=tt, mu=tV.n):
                if mu:
                    s.wait_ge(tV.h, mu)
                dS.inc(s.dma_start(out=ft[:, :], in_=self.s_ffnT[:, tt * 128:(tt + 1) * 128],
                                   transpose=True), 16)
                dS.inc(s.dma_start(out=xm[:, :], in_=self.s_xmid[tt * 128:(tt + 1) * 128, :]), 16)
            self.sec("sync", f_sy)
            m_ld = dS.n

            def f_v(v, m=m_ld, md=dC.n):
                v.wait_ge(dS.h, m)
                if md:
                    v.wait_ge(dC.h, md)
                v.tensor_tensor(out=ot[:, :], in0=xm[:, :], in1=ft[:, :], op=ALU.add)
                tV.dinc(v)
            self.sec("vector", f_v)
            m_o = tV.n

            def f_sc(sc, tt=tt, m=m_o):
                sc.wait_ge(tV.h, m)
                dC.inc(sc.dma_start(out=self.o_out[tt * 128:(tt + 1) * 128, :], in_=ot[:, :]), 16)
            self.sec("scalar", f_sc)
        ar32.pop(); ar16.pop()

    def build(self):
        import os
        stop = os.environ.get("K_STOP", "")
        nc = self.nc
        with nc.Block() as block:
            self.block = block

            def _stop(name):
                if stop == name:
                    self.barrier()
                    raise _StopBuild()
            self.stage_consts()
            self.ar16.push()
            self.hlnT = self.ar16.take(NKD * TL)
            self.wmbuf = [self.ar16.take(2048), self.ar16.take(2048)]
            try:
                self.stage_ln(self.i_x, self.s_hln, self.hlnT, TL, with_pre=True)
                self.m_hlnT = self.dS.n
                _stop("ln")
                self.stage_dt()
                _stop("dt")
                self.stage_xbc()
                self.barrier()
                _stop("xbc")
                self.stage_ssd1()
                self.ar16.pop()
                self.barrier()
                _stop("ssd1")
                self.stage_ag()
                self.barrier()
                _stop("ag")
                self.stage_pcomb()
                _stop("pcomb")
                self.stage_gate_wout()
                self.barrier()
                _stop("gate")
                self.stage_resid1()
                self.barrier()
                _stop("resid")
                self.ar16.push()
                self.h2T = self.ar16.take(NKD * T)
                self.stage_ln(self.s_xmid, self.s_h2, self.h2T, T, with_pre=False)
                self.barrier()
                _stop("ln2")
                self.stage_ffn()
                self.barrier()
                _stop("ffn")
                self.stage_final()
                self.barrier()
            except _StopBuild:
                pass
        return nc


def _sel2t():
    s = np.zeros((64, 32 * 128), np.float32)
    for g in range(32):
        s[2 * g, g * 128:g * 128 + 64] = 1.0
        s[2 * g + 1, g * 128 + 64:(g + 1) * 128] = 1.0
    return s.astype(BF)


def make_inputs(ins, core):
    b, q = core // 4, core % 4
    x = np.asarray(ins["L0_attn_hidden_states"])
    t0 = q * T
    main = x[b, t0:t0 + T]
    pre = np.zeros((PRE, D), np.float32) if q == 0 else x[b, t0 - PRE:t0]
    xs = np.ascontiguousarray(np.concatenate([pre, main], axis=0), dtype=np.float32)

    w_in = np.asarray(ins["L0_attn_W_in"]).astype(np.float32)
    w_in_p = np.zeros((D, 67 * 128), np.float32)
    w_in_p[:, :DPROJ] = w_in
    w_in_t = np.ascontiguousarray(
        w_in_p.reshape(NKD, 128, 67, 128).transpose(2, 1, 0, 3).reshape(67, 128, 2048))
    w_out_t = np.ascontiguousarray(
        np.asarray(ins["L0_attn_W_out"]).astype(np.float32)
        .reshape(32, 128, 16, 128).transpose(2, 1, 0, 3).reshape(16, 128, 4096))
    wg = np.asarray(ins["L0_ffn_w_gating"]).astype(np.float32)
    wg0_t = np.ascontiguousarray(
        wg[0].reshape(16, 128, 64, 128).transpose(2, 1, 0, 3).reshape(64, 128, 2048))
    wg1_t = np.ascontiguousarray(
        wg[1].reshape(16, 128, 64, 128).transpose(2, 1, 0, 3).reshape(64, 128, 2048))
    wl_t = np.ascontiguousarray(
        np.asarray(ins["L0_ffn_w_linear"]).astype(np.float32)
        .reshape(64, 128, 16, 128).transpose(2, 1, 0, 3).reshape(16, 128, 8192))

    ind = np.zeros(8, np.float32)
    chainM = np.zeros((8, 8), np.float32)
    for qp in range(4):
        src = b * 4 + qp
        if qp < q:
            ind[src] = 1.0
            for k in range(qp + 1, q):
                chainM[src, b * 4 + k] = 1.0
    chainMT = np.ascontiguousarray(chainM.T)

    return {
        "x_slice": xs,
        "w_in_t": w_in_t,
        "w_out_t": w_out_t,
        "w_g0_t": wg0_t,
        "w_g1_t": wg1_t,
        "w_lin_t": wl_t,
        "conv_w": np.ascontiguousarray(np.asarray(ins["L0_attn_conv_w"]).astype(np.float32)),
        "conv_b": np.ascontiguousarray(np.asarray(ins["L0_attn_conv_b"]).astype(np.float32)),
        "A_vec": np.ascontiguousarray(np.asarray(ins["L0_attn_A"]).astype(np.float32)),
        "D_vec": np.ascontiguousarray(np.asarray(ins["L0_attn_D"]).astype(np.float32)),
        "dt_bias": np.ascontiguousarray(np.asarray(ins["L0_attn_dt_bias"]).astype(np.float32)),
        "norm_w": np.ascontiguousarray(np.asarray(ins["L0_attn_norm_w"]).astype(np.float32)),
        "id_bf": np.eye(128, dtype=BF),
        "id_f32": np.eye(128, dtype=np.float32),
        "sel2": np.stack([np.concatenate([np.ones(64), np.zeros(64)]),
                          np.concatenate([np.zeros(64), np.ones(64)])]).astype(BF),
        "sel2t": _sel2t(),
        "chainMT": chainMT,
        "ind_mask": ind,
    }


def kernel(**inputs):
    b = Builder(debug=DEBUG)
    nc = b.build()
    in_maps = [make_inputs(inputs, c) for c in range(8)]
    res = run_bass_kernel_spmd(nc, in_maps, core_ids=list(range(8)))
    out = np.zeros((B, S, D), np.float32)
    for c in range(8):
        bb, q = c // 4, c % 4
        out[bb, q * T:(q + 1) * T] = res.results[c]["out"]
    return out


# version 06
# speedup vs baseline: 1.0922x; 1.0922x over previous
import numpy as np
import ml_dtypes
from contextlib import ExitStack
import os as _os

import concourse.bass as bass
import concourse.mybir as mybir
PM = mybir.MatmulPerfMode
from concourse.bass_utils import run_bass_kernel_spmd

F32 = mybir.dt.float32
BF16 = mybir.dt.bfloat16
F8 = mybir.dt.float8e4
AF = mybir.ActivationFunctionType
ALU = mybir.AluOpType
AX = mybir.AxisListType
BF = ml_dtypes.bfloat16

B, S, D = 2, 4096, 2048
T = 1024
PRE = 16
TL = T + PRE
NH, HD, DST = 64, 64, 128
DIN = NH * HD
DCONV = DIN + 2 * DST
DPROJ = 2 * DIN + 2 * DST + NH
CH = 128
NCH = T // CH
DFF = 8192
EPS = 1e-5
NKD = D // 128
NXBC = DCONV // 128
NZ = DIN // 128
NFF = DFF // 128

DEBUG = ()


class _StopBuild(Exception):
    pass


class Ctr:
    def __init__(self, stack, nc, name):
        self.h = stack.enter_context(nc.semaphore(name))
        self.n = 0

    def inc(self, inst, k=1):
        inst.then_inc(self.h, k)
        self.n += k
        return self.n

    def dinc(self, eng, k=1):
        eng.maybe_drain_then_inc((self.h, k), fusable=False)
        self.n += k
        return self.n


class Arena:
    def __init__(self, nc, name, cols, dtype):
        self.t = nc.alloc_sbuf_tensor(name, [128, cols], dtype)
        self.cols = cols
        self.off = 0
        self.marks = []

    def take(self, cols, rows=128):
        cols = (cols + 15) // 16 * 16
        assert self.off + cols <= self.cols, f"{self.off}+{cols}>{self.cols}"
        ap = self.t[0:rows, self.off:self.off + cols]
        self.off += cols
        return ap

    def push(self):
        self.marks.append(self.off)

    def pop(self):
        self.off = self.marks.pop()


class Builder:
    def __init__(self, debug=()):
        self.debug = debug
        nc = bass.Bass("TRN2", target_bir_lowering=False, detect_race_conditions=False)
        self.nc = nc
        self.stack = ExitStack()

        dram = nc.dram_tensor
        self.i_x = dram("x_slice", [TL, D], F32, kind="ExternalInput")
        self.i_win = dram("w_in_t", [67, 128, 2048], F32, kind="ExternalInput")
        self.i_wout = dram("w_out_t", [16, 128, 4096], F32, kind="ExternalInput")
        self.i_wg0 = dram("w_g0_t", [64, 128, 2048], F32, kind="ExternalInput")
        self.i_wg1 = dram("w_g1_t", [64, 128, 2048], F32, kind="ExternalInput")
        self.i_wlin = dram("w_lin_t", [16, 128, 8192], F32, kind="ExternalInput")
        self.i_convw = dram("conv_w", [DCONV, 4], F32, kind="ExternalInput")
        self.i_convb = dram("conv_b", [DCONV], F32, kind="ExternalInput")
        self.i_A = dram("A_vec", [NH], F32, kind="ExternalInput")
        self.i_D = dram("D_vec", [NH], F32, kind="ExternalInput")
        self.i_dtb = dram("dt_bias", [NH], F32, kind="ExternalInput")
        self.i_normw = dram("norm_w", [DIN], F32, kind="ExternalInput")
        self.i_id_bf = dram("id_bf", [128, 128], BF16, kind="ExternalInput")
        self.i_id_f32 = dram("id_f32", [128, 128], F32, kind="ExternalInput")
        self.i_sel2 = dram("sel2", [2, 128], BF16, kind="ExternalInput")
        self.i_sel2t = dram("sel2t", [64, 32 * 128], BF16, kind="ExternalInput")
        self.i_chainMT = dram("chainMT", [8, 8], F32, kind="ExternalInput")
        self.i_ind = dram("ind_mask", [8], F32, kind="ExternalInput")

        self.o_out = dram("out", [T, D], F32, kind="ExternalOutput")

        _sk = {"kind": "ExternalOutput"} if _os.environ.get("K_SPILLOUT") else {}
        self.s_hln = dram("s_hln", [TL, D], BF16, **_sk)
        self.s_xfm = dram("s_xfm", [DIN, T], BF16, **_sk)
        self.s_z = dram("s_z", [DIN, T], BF16, **_sk)
        self.s_y = dram("s_y", [DIN, T], BF16, **_sk)
        self.s_lc = dram("s_lc", [NCH, 128, DIN], BF16, **_sk)
        self.s_xmid = dram("s_xmid", [T, D], F32, **_sk)
        self.s_h2 = dram("s_h2", [T, D], BF16, **_sk)
        self.s_attnT = dram("s_attnT", [D, T], BF16, **_sk)
        self.s_ffnT = dram("s_ffnT", [D, T], BF16, **_sk)
        self.s_act = dram("s_act", [DFF, T], F8, **_sk)
        self.cc_in = dram("cc_in", [128, DIN + 64], F32)
        self.cc_out = dram("cc_out", [8 * 128, DIN + 64], F32)
        if self.debug:
            self.d_dtx = dram("d_dtx", [128, DIN], BF16)
            self.d_dtxd = dram("d_dtxd", [128, DIN], BF16)
            self.d_decay = dram("d_decay", [128, 64], F32)
            self.d_csT = dram("d_csT", [128, 64], F32)
            self.d_bmt = dram("d_bmt", [128, 128], BF16)
            self.d_g = dram("d_g", [128, 128], BF16)
            self.d_eg = dram("d_eg", [128, 512], BF16)
            self.d_xtok = dram("d_xtok", [128, DIN], BF16)
            self.d_dt = dram("d_dt", [64, T], F32)
            self.d_P = dram("d_P", [128, DIN], F32)
            self.d_prev0 = dram("d_prev0", [128, DIN], BF16)
            self.d_cs = dram("d_cs", [64, T], F32)
            self.d_eg2 = dram("d_eg2", [128, 512], BF16)
            self.d_cshi = dram("d_cshi", [64, T], BF16)
            self.d_cslo = dram("d_cslo", [64, T], BF16)

        A = nc.alloc_sbuf_tensor
        self.id_bf = A("sid_bf", [128, 128], BF16)
        self.id_f32 = A("sid_f32", [128, 128], F32)
        self.sel2 = A("ssel2", [2, 128], BF16)
        self.sel2t = A("ssel2t", [64, 32 * 128], BF16)
        self.ones1f = A("sones1f", [1, 128], F32)
        self.ones1b = A("sones1b", [1, 128], BF16)
        self.onescolb = A("sonescolb", [128, 1], BF16)
        self.convW = A("sconvW", [128, 4 * NXBC], F32)
        self.convB = A("sconvB", [128, NXBC], F32)
        self.acol = A("sacol", [NH, 1], F32)
        self.dtbcol = A("sdtbcol", [NH, 1], F32)
        self.dexp2 = A("sdexp2", [64, 64], F32)
        self.normw = A("snormw", [1, DIN], BF16)
        self.chainMT = A("schainMT", [8, 8], F32)
        self.indcol = A("sindcol", [8, 1], F32)
        self.zerocol = A("szerocol", [64, 1], F32)
        self.epscol = A("sepscol", [128, 1], F32)
        self.dt_sb = A("sdt", [64, T], F32)
        self.cs_sb = A("scs", [64, T], F32)
        self.cshi = A("scshi", [64, T], BF16)
        self.cslo = A("scslo", [64, T], BF16)
        self.expcs = A("sexpcs", [64, T], BF16)
        self.stab = A("sstab", [1, 64 * (NCH + 1)], F32)
        self.cdrow = A("scdrow", [1, 64 * NCH], F32)
        self.expcum = A("sexpcum", [1, 64 * NCH], F32)
        self.coeff = A("scoeff", [8, 64], F32)
        self.coeff0 = A("scoeff0", [8, 64], F32)
        self.smat = A("ssmat", [8, 64], F32)
        self.b_sb = A("sb_sb", [128, T], BF16)
        self.c_sb = A("sc_sb", [128, T], BF16)
        self.g_sb = A("sg_sb", [128, T], BF16)

        self.ar32 = Arena(nc, "ar32", 12544, F32)
        self.ar16 = Arena(nc, "ar16", 55 * 1024, BF16)

        self.pb = [nc.alloc_psum_tensor(f"pb{i}", [128, 512], F32) for i in range(8)]

        st = self.stack
        self.dS = Ctr(st, nc, "dS")
        self.dC = Ctr(st, nc, "dC")
        self.dG = Ctr(st, nc, "dG")
        self.tT = Ctr(st, nc, "tT")
        self.tV = Ctr(st, nc, "tV")
        self.tA = Ctr(st, nc, "tA")
        self.tG = Ctr(st, nc, "tG")
        self.cc = Ctr(st, nc, "ccs")
        self.bar = Ctr(st, nc, "bar")
        self.sA = Ctr(st, nc, "sa")
        self.sM = Ctr(st, nc, "sm")
        self.sY = Ctr(st, nc, "sy")
        self.sS = Ctr(st, nc, "ss")
        self.sZ = Ctr(st, nc, "sz")
        self.sW = Ctr(st, nc, "sw")
        self.sCH = Ctr(st, nc, "sch")

        self.block = None
        self._nbar = 0
        self.dbg = {}

    def sec(self, eng, fn):
        getattr(self.block, eng)(fn)

    def barrier(self):
        self._nbar += 1
        target = self._nbar * 5
        dSn, dCn, dGn = self.dS.n, self.dC.n, self.dG.n
        b = self.bar

        def mk(dh, dn, own_h, own_n):
            def f(eng):
                if dh is not None:
                    eng.wait_ge(dh, dn)
                if own_h is not None:
                    eng.wait_ge(own_h, own_n)
                eng.sem_inc(b.h, 1)
                eng.wait_ge(b.h, target)
            return f
        self.sec("sync", mk(self.dS.h, dSn, None, 0))
        self.sec("scalar", mk(self.dC.h, dCn, self.tA.h, self.tA.n))
        self.sec("gpsimd", mk(self.dG.h, dGn, self.tG.h, self.tG.n))
        self.sec("vector", mk(None, None, self.tV.h, self.tV.n))
        self.sec("tensor", mk(None, None, self.tT.h, self.tT.n))
        b.n = target

    def dump(self, name, ap, shape):
        if name not in self.debug:
            return
        self.barrier()
        o = self.nc.dram_tensor("dbg_" + name, list(shape), ap.dtype, kind="ExternalOutput")
        dG = self.dG

        def f(g):
            dG.inc(g.dma_start(out=o.ap(), in_=ap), 16)
            g.wait_ge(dG.h, dG.n)
        self.sec("gpsimd", f)
        self.barrier()

    def load_w(self, g, src_ap, dst, free_mark):
        if free_mark:
            g.wait_ge(self.tT.h, free_mark)
        return self.dG.inc(g.dma_start(out=dst, in_=src_ap), 16)

    def stage_consts(self):
        dS, dG, tV = self.dS, self.dG, self.tV

        def f_sync(s):
            dS.inc(s.dma_start(out=self.id_bf[:], in_=self.i_id_bf[:, :]), 16)
            dS.inc(s.dma_start(out=self.id_f32[:], in_=self.i_id_f32[:, :]), 16)
            dS.inc(s.dma_start(out=self.sel2[:], in_=self.i_sel2[:, :]), 16)
            dS.inc(s.dma_start(out=self.sel2t[:], in_=self.i_sel2t[:, :]), 16)
            dS.inc(s.dma_start(out=self.convW[:].rearrange("p (m k) -> p m k", k=4),
                               in_=bass.AP(self.i_convw, 0, [[4, 128], [512, NXBC], [1, 4]])), 16)
            dS.inc(s.dma_start(out=self.convB[:],
                               in_=bass.AP(self.i_convb, 0, [[1, 128], [128, NXBC]]),
                               allow_slow_non_contiguous=True), 16)
            dS.inc(s.dma_start(out=self.acol[:, 0], in_=self.i_A[:]), 16)
            dS.inc(s.dma_start(out=self.dtbcol[:, 0], in_=self.i_dtb[:]), 16)
            dS.inc(s.dma_start(out=self.dexp2[:],
                               in_=bass.AP(self.i_D, 0, [[0, 64], [1, 64]])), 16)
            dS.inc(s.dma_start(out=self.chainMT[:], in_=self.i_chainMT[:, :]), 16)
            dS.inc(s.dma_start(out=self.indcol[:, 0], in_=self.i_ind[:]), 16)
        self.sec("sync", f_sync)

        def f_gp(g):
            dG.inc(g.dma_start(out=self.normw[0:1, :],
                               in_=bass.AP(self.i_normw, 0, [[0, 1], [1, DIN]])), 16)
        self.sec("gpsimd", f_gp)

        def f_vec(v):
            v.memset(self.ones1f[:], 1.0)
            v.memset(self.ones1b[:], 1.0)
            v.memset(self.onescolb[:], 1.0)
            v.memset(self.zerocol[:], 0.0)
            v.memset(self.epscol[:], EPS)
            v.memset(self.stab[:, 0:64], 0.0)
            tV.dinc(v)
        self.sec("vector", f_vec)

    def stage_ln(self, src_dram, dst_dram, dstT_sb, rows, with_pre):
        ar32, ar16 = self.ar32, self.ar16
        ar32.push(); ar16.push()
        xt = [ar32.take(D), ar32.take(D)]
        sq = ar32.take(D)
        s1 = ar32.take(16)
        hls = [ar16.take(D), ar16.take(D)]
        dS, dC, tV, tA = self.dS, self.dC, self.tV, self.tA

        tiles = ([(0, PRE)] if with_pre else []) + \
                [((PRE if with_pre else 0) + t * 128, 128)
                 for t in range((rows - (PRE if with_pre else 0)) // 128)]

        m_lastuse = [0, 0]
        m_dma = [0, 0]
        m_red = [0, 0]

        for i, (r0, nr) in enumerate(tiles):
            par = i % 2
            x = xt[par][0:nr, :]
            hl = hls[par][0:nr, :]

            def f_ld(s, r0=r0, nr=nr, x=x, mf=m_lastuse[par]):
                if mf:
                    s.wait_ge(tA.h, mf)
                dS.inc(s.dma_start(out=x, in_=src_dram[r0:r0 + nr, :]), 16)
            self.sec("sync", f_ld)
            m_ld = dS.n

            def f_sq(sc, x=x, nr=nr, m=m_ld, mr=m_red[0]):
                sc.wait_ge(dS.h, m)
                if mr:
                    sc.wait_ge(tV.h, mr)
                sc.activation(out=sq[0:nr, :], in_=x, func=AF.Square)
                tA.dinc(sc)
            self.sec("scalar", f_sq)
            m_sq = tA.n

            def f_v1(v, x=x, nr=nr, m=m_sq, ml=m_ld):
                v.wait_ge(dS.h, ml)
                v.reduce_sum(out=s1[0:nr, 0:1], in_=x, axis=AX.X)
                v.wait_ge(tA.h, m)
                v.reduce_sum(out=s1[0:nr, 1:2], in_=sq[0:nr, :], axis=AX.X)
                tV.dinc(v)
            self.sec("vector", f_v1)
            m_red[0] = tV.n
            m_v1 = tV.n

            def f_s2(sc, nr=nr, m=m_v1):
                sc.wait_ge(tV.h, m)
                sc.activation(out=s1[0:nr, 2:3], in_=s1[0:nr, 0:1], func=AF.Identity,
                              scale=1.0 / D)
                sc.activation(out=s1[0:nr, 7:8], in_=s1[0:nr, 0:1], func=AF.Identity,
                              scale=-1.0 / D)
                sc.activation(out=s1[0:nr, 3:4], in_=s1[0:nr, 2:3], func=AF.Square)
                sc.activation(out=s1[0:nr, 4:5], in_=s1[0:nr, 3:4], func=AF.Identity,
                              scale=-1.0, bias=self.epscol[0:nr, 0:1])
                sc.drain()
                sc.activation(out=s1[0:nr, 5:6], in_=s1[0:nr, 1:2], func=AF.Identity,
                              scale=1.0 / D, bias=s1[0:nr, 4:5])
                tA.dinc(sc)
            self.sec("scalar", f_s2)
            m_s2a = tA.n

            def f_v2(v, nr=nr, m=m_s2a):
                v.wait_ge(tA.h, m)
                v.reciprocal(out=s1[0:nr, 9:10], in_=s1[0:nr, 5:6])
                tV.dinc(v)
            self.sec("vector", f_v2)
            m_v2 = tV.n

            def f_s2b(sc, nr=nr, m=m_v2):
                sc.wait_ge(tV.h, m)
                sc.activation(out=s1[0:nr, 6:7], in_=s1[0:nr, 9:10], func=AF.Sqrt)
                sc.drain()
                sc.activation(out=s1[0:nr, 8:9], in_=s1[0:nr, 7:8], func=AF.Identity,
                              scale=s1[0:nr, 6:7])
                tA.dinc(sc)
            self.sec("scalar", f_s2b)

            def f_s3(sc, x=x, nr=nr, hl=hl, r0=r0, md=m_dma[par]):
                if md:
                    sc.wait_ge(dC.h, md)
                sc.activation(out=hl, in_=x, func=AF.Identity,
                              bias=s1[0:nr, 8:9], scale=s1[0:nr, 6:7])
                tA.dinc(sc)
                dC.inc(sc.dma_start(out=dst_dram[r0:r0 + nr, :], in_=hl), 16)
            self.sec("scalar", f_s3)
            m_lastuse[par] = tA.n
            m_dma[par] = dC.n

            if self.debug and with_pre and i == 0 and "lnpre" in self.debug:
                self.dump("lnpre_s1", s1[0:16, 0:8], [16, 8])
                self.dump("lnpre_hl", hls[0][0:16, :], [16, D])
                self.dump("lnpre_x", xt[0][0:16, :], [16, D])
                self.dump("lnpre_sq", sq[0:16, :], [16, D])

        m_all = self.dC.n

        def f_tr(s):
            s.wait_ge(dC.h, m_all)
            for fc in range(NKD):
                dS.inc(s.dma_start(out=dstT_sb[:, fc * rows:(fc + 1) * rows],
                                   in_=dst_dram[:, fc * 128:(fc + 1) * 128],
                                   transpose=True), 16)
        self.sec("sync", f_tr)
        ar32.pop(); ar16.pop()

    def stage_dt(self):
        ar32 = self.ar32
        ar32.push()
        tmp32 = ar32.take(T, rows=64)
        da32 = ar32.take(T, rows=64)
        dS, dG, tT, tV, tA, sW = self.dS, self.dG, self.tT, self.tV, self.tA, self.sW
        wm = self.wmbuf[0]
        pb = self.pb
        m_hlnT = self.m_hlnT

        def f_gp(g):
            self.load_w(g, self.i_win[66, :, :], wm, 0)
        self.sec("gpsimd", f_gp)
        m_w = dG.n
        self.wfree = [0, 0]

        def f_t(t):
            t.wait_ge(dS.h, m_hlnT)
            t.wait_ge(dG.h, m_w)
            li = None
            for tt in range(2):
                for k in range(NKD):
                    li = t.matmul(out=pb[tt][:, :], lhsT=wm[:, k * 128:(k + 1) * 128],
                                  rhs=self.hlnT[:, k * TL + PRE + tt * 512:k * TL + PRE + (tt + 1) * 512],
                                  start=(k == 0), stop=(k == NKD - 1))
            tT.inc(li)
        self.sec("tensor", f_t)
        self.wfree[0] = tT.n
        m_mm = tT.n

        import os as _os
        _dtcut = _os.environ.get("K_DT", "")

        def f_sc(sc):
            sc.wait_ge(tT.h, m_mm)
            if _dtcut == "mm":
                tA.inc(sc.activation(out=tmp32[:, 0:1], in_=self.dtbcol[:, 0:1], func=AF.Identity))
                return
            for tt in range(2):
                sc.activation(out=tmp32[:, tt * 512:(tt + 1) * 512], in_=pb[tt][0:64, :],
                              func=AF.Exp, bias=self.dtbcol[:, 0:1])
            tA.inc(sc.activation(out=self.dt_sb[:], in_=tmp32[:, :], func=AF.Ln, bias=1.0))
        self.sec("scalar", f_sc)
        self.m_dtact = tA.n

        def f_v(v):
            v.wait_ge(tA.h, self.m_dtact)
            if _dtcut in ("mm", "act"):
                v.memset(self.dt_sb[:], 0.01)
                v.memset(self.cs_sb[:], -0.01)
            else:
                v.tensor_scalar(out=da32[:, :], in0=self.dt_sb[:], scalar1=self.acol[:, 0:1],
                                scalar2=None, op0=ALU.mult)
                if _dtcut == "noscan":
                    v.tensor_copy(out=self.cs_sb[:], in_=da32[:, :])
                else:
                    v.drain()
                    for c in range(NCH):
                        v.tensor_tensor_scan(out=self.cs_sb[0:64, c * CH:(c + 1) * CH],
                                             data0=da32[:, c * CH:(c + 1) * CH],
                                             data1=self.zerocol[:, 0:1].broadcast_to([64, CH]),
                                             initial=0.0, op0=ALU.add, op1=ALU.add)
                    v.drain()
            v.tensor_copy(out=self.cshi[:], in_=self.cs_sb[:])
            v.drain()
            v.tensor_tensor(out=self.cslo[:], in0=self.cs_sb[:], in1=self.cshi[:],
                                   op=ALU.subtract)
            tV.dinc(v)
        self.sec("vector", f_v)
        self.m_cs = tV.n

        def f_sc2(sc):
            sc.wait_ge(tV.h, self.m_cs)
            tA.inc(sc.activation(out=self.expcs[:], in_=self.cs_sb[:], func=AF.Exp))
        self.sec("scalar", f_sc2)
        ar32.pop()

    def stage_xbc(self):
        ar32, ar16 = self.ar32, self.ar16
        ar32.push(); ar16.push()
        raw = [ar32.take(TL + 8), ar32.take(TL + 8)]
        cvt = [ar32.take(T), ar32.take(T)]
        xcs = [ar16.take(T), ar16.take(T)]
        dS, dC, dG, tT, tV, tA, sW = self.dS, self.dC, self.dG, self.tT, self.tV, self.tA, self.sW
        pb = self.pb
        m_hlnT = self.m_hlnT
        m_evac = [0, 0]
        m_silu = [0, 0]
        m_xdma = [0, 0]

        for mi in range(NXBC):
            par = mi % 2
            wm = self.wmbuf[par]
            psC, psA, psB = (pb[0], pb[1], pb[2]) if par == 0 else (pb[3], pb[4], pb[5])

            def f_gp(g, mi=mi, wm=wm, fm=self.wfree[par]):
                self.load_w(g, self.i_win[32 + mi, :, :], wm, fm)
            self.sec("gpsimd", f_gp)
            m_w = dG.n

            def f_t(t, wm=wm, psC=psC, psA=psA, psB=psB, m_w=m_w, me=m_evac[par], first=(mi < 2)):
                t.wait_ge(dS.h, m_hlnT)
                t.wait_ge(dG.h, m_w)
                if first:
                    t.wait_ge(tA.h, self.m_dtact)
                if me:
                    t.wait_ge(tV.h, me)
                li = None
                for k in range(NKD):
                    for ps, c0, w in ((psC, 0, PRE), (psA, PRE, 512), (psB, PRE + 512, 512)):
                        li = t.matmul(out=ps[:, 0:w], lhsT=wm[:, k * 128:(k + 1) * 128],
                                      rhs=self.hlnT[:, k * TL + c0:k * TL + c0 + w],
                                      start=(k == 0), stop=(k == NKD - 1))
                tT.inc(li)
            self.sec("tensor", f_t)
            self.wfree[par] = tT.n
            m_mm = tT.n

            def f_v1(v, r=raw[par], psC=psC, psA=psA, psB=psB, m=m_mm, msil=m_silu[par]):
                v.wait_ge(tT.h, m)
                if msil:
                    v.wait_ge(tA.h, msil)
                v.tensor_copy(out=r[:, 0:16], in_=psC[:, 0:16])
                v.tensor_copy(out=r[:, 16:528], in_=psA[:, :])
                v.tensor_copy(out=r[:, 528:1040], in_=psB[:, :])
                tV.dinc(v)
            self.sec("vector", f_v1)
            m_evac[par] = tV.n
            m_ev = tV.n

            def f_s1(sc, r=raw[par], cv=cvt[par], m=m_ev, mi=mi, msil=m_silu[par]):
                sc.wait_ge(tV.h, m)
                tA.inc(sc.activation(out=cv[:, :], in_=r[:, 13:1037], func=AF.Identity,
                                     scale=self.convW[:, 4 * mi:4 * mi + 1]))
            self.sec("scalar", f_s1)
            m_t0 = tA.n

            def f_v2(v, r=raw[par], cv=cvt[par], m=m_t0, mi=mi):
                v.wait_ge(tA.h, m)
                li = None
                for k in (1, 2, 3):
                    li = v.scalar_tensor_tensor(out=cv[:, :], in0=r[:, 13 + k:1037 + k],
                                                scalar=self.convW[:, 4 * mi + k:4 * mi + k + 1],
                                                in1=cv[:, :], op0=ALU.mult, op1=ALU.add)
                tV.inc(li)
            self.sec("vector", f_v2)
            m_taps = tV.n

            if mi < 32:
                dst = xcs[par]
            elif mi == 32:
                dst = self.b_sb[:, :]
            else:
                dst = self.c_sb[:, :]

            def f_s2(sc, dst=dst, cv=cvt[par], m=m_taps, mi=mi, md=m_xdma[par]):
                sc.wait_ge(tV.h, m)
                if mi < 32 and md:
                    sc.wait_ge(dC.h, md)
                sc.activation(out=dst, in_=cv[:, :], func=AF.Silu,
                              bias=self.convB[:, mi:mi + 1])
                tA.dinc(sc)
                if mi < 32:
                    dC.inc(sc.dma_start(out=self.s_xfm[mi * 128:(mi + 1) * 128, :], in_=dst), 16)
            self.sec("scalar", f_s2)
            m_silu[par] = tA.n
            if mi < 32:
                m_xdma[par] = dC.n
        self.m_bc = self.tA.n
        ar32.pop(); ar16.pop()

    def emit_z_unit(self, zm):
        dS, dC, dG, tT, tV, sW, sZ = self.dS, self.dC, self.dG, self.tT, self.tV, self.sW, self.sZ
        pb2 = self.pb[2]
        par = (35 + zm) % 2
        wm = self.wmbuf[par]

        def f_gp(g, wm=wm, zm=zm, fm=self.wfree[par]):
            self.load_w(g, self.i_win[zm, :, :], wm, fm)
        self.sec("gpsimd", f_gp)
        m_w = dG.n

        for tt in range(2):
            def f_t(t, wm=wm, m_w=m_w, tt=tt, mz=self.sZ.n):
                t.wait_ge(dG.h, m_w)
                if mz:
                    t.wait_ge(sZ.h, mz)
                li = None
                for k in range(NKD):
                    li = t.matmul(out=pb2[:, :], lhsT=wm[:, k * 128:(k + 1) * 128],
                                  rhs=self.hlnT[:, k * TL + PRE + tt * 512:k * TL + PRE + (tt + 1) * 512],
                                  start=(k == 0), stop=(k == NKD - 1))
                tT.inc(li)
            self.sec("tensor", f_t)
            if tt == 1:
                self.wfree[par] = tT.n
            m_mm = tT.n

            def f_v(v, tt=tt, m=m_mm, md=self._z_dma):
                v.wait_ge(tT.h, m)
                if md:
                    v.wait_ge(dC.h, md)
                v.tensor_copy(out=self.zstg[:, tt * 512:(tt + 1) * 512], in_=pb2[:, :])
                sZ.dinc(v)
            self.sec("vector", f_v)

        def f_s(sc, zm=zm, me=self.sZ.n):
            sc.wait_ge(sZ.h, me)
            dC.inc(sc.dma_start(out=self.s_z[zm * 128:(zm + 1) * 128, :], in_=self.zstg[:]), 16)
        self.sec("scalar", f_s)
        self._z_dma = dC.n

    def stage_ssd1(self):
        ar32, ar16 = self.ar32, self.ar16
        ar32.push(); ar16.push()
        csT = ar32.take(64)
        cslast = ar32.take(64, rows=1)
        negcsT = ar32.take(64)
        dtT = ar32.take(64)
        decay = ar32.take(64)
        dtdec = ar32.take(64)
        self.carry = [ar32.take(DIN), ar32.take(DIN)]
        eg32 = [ar32.take(512), ar32.take(512)]
        x_tok = ar16.take(DIN)
        dtx = ar16.take(DIN)
        dtxd = ar16.take(DIN)
        xfmc = ar16.take(2 * DIN, rows=64)
        ystg = ar16.take(2 * DIN, rows=64)
        lcstg = ar16.take(DIN)
        eg = [ar16.take(512), ar16.take(512)]
        self.zstg = ar16.take(T)
        BmT = ar16.take(128)

        dS, dC, dG = self.dS, self.dC, self.dG
        tT, tV, tA, tG = self.tT, self.tV, self.tA, self.tG
        sA, sM, sY, sS, sCH = self.sA, self.sM, self.sY, self.sS, self.sCH
        pb = self.pb
        self._z_dma = 0
        z_next = [0]
        import os as _os
        _cut = _os.environ.get("K_SSD", "")
        _do_quads = _cut not in ("pre", "loads", "tp", "repl", "dtx")
        _do_states = _cut not in ("pre", "quads", "loads", "tp", "repl", "dtx")
        _do_z = _cut not in ("pre", "quads", "states", "loads", "tp", "repl", "dtx")
        _pre_lvl = {"loads": 0, "tp": 1, "repl": 2, "dtx": 3, "g": 4, "cdr": 5, "lc": 6}.get(_cut, 9)
        _do_quads = _do_quads and _pre_lvl >= 9
        _do_states = _do_states and _pre_lvl >= 9
        _do_z = _do_z and _pre_lvl >= 9

        def f_v0(v):
            v.memset(self.carry[0][:, :], 0.0)
            tV.dinc(v)
        self.sec("vector", f_v0)

        def f_g0(g):
            fr = self.stack.enter_context(g.register("fillreg"))
            g.reg_mov(fr, float(-1e30))
            self.fill_val = g.snap(fr)
        self.sec("gpsimd", f_g0)

        m_ystg_dma = 0
        m_lc_dma = 0
        quad_idx = [0]

        for c in range(NCH):
            cur, nxt = self.carry[c % 2], self.carry[(c + 1) % 2]
            csl = slice(c * CH, (c + 1) * CH)

            def f_sy(s, c=c):
                if c >= 1:
                    s.wait_ge(sS.h, c * 8)
                dS.inc(s.dma_start(
                    out=xfmc[:].rearrange("p (h i) -> p h i", h=64),
                    in_=bass.AP(self.s_xfm, c * CH, [[T, 64], [64 * T, 64], [1, CH]])), 16)
                dS.inc(s.dma_start(out=x_tok[:], in_=self.s_xfm[:, c * CH:(c + 1) * CH],
                                   transpose=True), 16)
            self.sec("sync", f_sy)
            m_ld = dS.n

            if _pre_lvl < 1:
                def f_skip(v, cur=cur, nxt=nxt):
                    for sl8 in range(8):
                        v.tensor_copy(out=nxt[:, sl8 * 512:(sl8 + 1) * 512],
                                                  in_=cur[:, sl8 * 512:(sl8 + 1) * 512])
                        self.sS.dinc(v)
                self.sec("vector", f_skip)
                continue

            def f_t1(t, csl=csl, c=c):
                t.wait_ge(tV.h, self.m_cs)
                if c >= 1:
                    t.wait_ge(sS.h, c * 8)
                t.transpose(out=pb[3][0:128, 0:64], in_=self.cs_sb[0:64, csl],
                            identity=self.id_f32[0:64, 0:64])
                t.transpose(out=pb[3][0:128, 64:128], in_=self.dt_sb[0:64, csl],
                            identity=self.id_f32[0:64, 0:64])
                tT.inc(t.transpose(out=pb[3][0:1, 448:512],
                            in_=self.cs_sb[0:64, (c + 1) * CH - 1:(c + 1) * CH],
                            identity=self.id_f32[0:64, 0:64]))
            self.sec("tensor", f_t1)
            m_t1 = tT.n

            def f_v1(v, m=m_t1):
                v.wait_ge(tT.h, m)
                v.tensor_copy(out=csT[:], in_=pb[3][:, 0:64])
                v.tensor_copy(out=dtT[:], in_=pb[3][:, 64:128])
                v.tensor_copy(out=cslast[0:1, :], in_=pb[3][0:1, 448:512])
                v.tensor_scalar(out=negcsT[:], in0=csT[:], scalar1=-1.0,
                                       scalar2=None, op0=ALU.mult)
                tV.dinc(v)
            self.sec("vector", f_v1)
            m_v1 = tV.n

            if _pre_lvl < 2:
                def f_skip(v, cur=cur, nxt=nxt):
                    for sl8 in range(8):
                        v.tensor_copy(out=nxt[:, sl8 * 512:(sl8 + 1) * 512],
                                                  in_=cur[:, sl8 * 512:(sl8 + 1) * 512])
                        self.sS.dinc(v)
                self.sec("vector", f_skip)
                continue

            def f_t2(t, m=m_v1):
                t.wait_ge(tV.h, m)
                tT.inc(t.matmul(out=pb[3][:, 128:192], lhsT=self.ones1f[:], rhs=cslast[0:1, :],
                                start=True, stop=True))
            self.sec("tensor", f_t2)
            m_t2 = tT.n

            def f_v2(v, m=m_t2):
                v.wait_ge(tT.h, m)
                v.tensor_tensor(out=decay[:], in0=pb[3][:, 128:192], in1=csT[:],
                                       op=ALU.subtract)
                tV.dinc(v)
            self.sec("vector", f_v2)
            m_v2 = tV.n

            def f_s1(sc, c=c, m=m_v2):
                sc.wait_ge(tV.h, m)
                sc.activation(out=decay[:], in_=decay[:], func=AF.Exp)
                tA.inc(sc.activation(out=self.cdrow[0:1, c * 64:(c + 1) * 64],
                                     in_=cslast[0:1, :], func=AF.Exp))
            self.sec("scalar", f_s1)
            m_s1 = tA.n

            if _pre_lvl < 3:
                def f_skip(v, cur=cur, nxt=nxt):
                    for sl8 in range(8):
                        v.tensor_copy(out=nxt[:, sl8 * 512:(sl8 + 1) * 512],
                                                  in_=cur[:, sl8 * 512:(sl8 + 1) * 512])
                        self.sS.dinc(v)
                self.sec("vector", f_skip)
                continue

            def f_v3(v, c=c, m=m_s1, mld=m_ld):
                v.wait_ge(tA.h, m)
                v.tensor_tensor(out=self.stab[0:1, (c + 1) * 64:(c + 2) * 64],
                                in0=self.stab[0:1, c * 64:(c + 1) * 64],
                                in1=cslast[0:1, :], op=ALU.add)
                v.tensor_tensor(out=dtdec[:], in0=dtT[:], in1=decay[:], op=ALU.mult)
                v.wait_ge(dS.h, mld)
                v.tensor_tensor(out=dtx[:].rearrange("p (h q) -> p h q", h=64),
                                in0=x_tok[:].rearrange("p (h q) -> p h q", h=64),
                                in1=dtT[:, :, None].broadcast_to([128, 64, 64]), op=ALU.mult)
                v.tensor_tensor(out=dtxd[:].rearrange("p (h q) -> p h q", h=64),
                                       in0=x_tok[:].rearrange("p (h q) -> p h q", h=64),
                                       in1=dtdec[:, :, None].broadcast_to([128, 64, 64]),
                                       op=ALU.mult)
                tV.dinc(v)
            self.sec("vector", f_v3)
            m_dtx = tV.n

            if self.debug and c == 0:
                def f_dbg1(sc, m=m_dtx):
                    sc.wait_ge(tV.h, m)
                    dC.inc(sc.dma_start(out=self.d_dtx.ap(), in_=dtx[:, :]), 16)
                    dC.inc(sc.dma_start(out=self.d_dtxd.ap(), in_=dtxd[:, :]), 16)
                    dC.inc(sc.dma_start(out=self.d_decay.ap(), in_=decay[:, :]), 16)
                    dC.inc(sc.dma_start(out=self.d_csT.ap(), in_=csT[:, :]), 16)
                    dC.inc(sc.dma_start(out=self.d_xtok.ap(), in_=x_tok[:, :]), 16)
                    dC.inc(sc.dma_start(out=self.d_dt.ap(), in_=self.dt_sb[:, :]), 16)
                    dC.inc(sc.dma_start(out=self.d_cs.ap(), in_=self.cs_sb[:, :]), 16)
                self.sec("scalar", f_dbg1)

            if _pre_lvl < 4:
                def f_skip(v, cur=cur, nxt=nxt):
                    for sl8 in range(8):
                        v.tensor_copy(out=nxt[:, sl8 * 512:(sl8 + 1) * 512],
                                                  in_=cur[:, sl8 * 512:(sl8 + 1) * 512])
                        self.sS.dinc(v)
                self.sec("vector", f_skip)
                continue

            def f_syb(s, csl=csl, c=c, m=m_dtx):
                s.wait_ge(tV.h, m)
                dS.inc(s.dma_start(out=BmT[:], in_=self.b_sb[:, csl], transpose=True), 16)
            self.sec("sync", f_syb)
            m_bmt = dS.n

            def f_t3(t, csl=csl, m=m_dtx):
                t.wait_ge(tA.h, self.m_bc)
                t.wait_ge(tV.h, m)
                tT.inc(t.matmul(out=pb[3][:, 192:320], lhsT=self.b_sb[:, csl], rhs=self.c_sb[:, csl],
                                start=True, stop=True))
            self.sec("tensor", f_t3)
            m_t3 = tT.n

            def f_v4(v, c=c, m=m_t3):
                v.wait_ge(tT.h, m)
                v.tensor_copy(out=self.g_sb[:, c * CH:(c + 1) * CH], in_=pb[3][:, 192:320])
                tV.dinc(v)
            self.sec("vector", f_v4)
            m_v4 = tV.n
            if self.debug and c == 0:
                def f_dbg2(sc, m=m_v4):
                    sc.wait_ge(tV.h, m)
                    dC.inc(sc.dma_start(out=self.d_bmt.ap(), in_=BmT[:, :]), 16)
                    dC.inc(sc.dma_start(out=self.d_g.ap(), in_=self.g_sb[:, 0:128]), 16)
                self.sec("scalar", f_dbg2)

            if _pre_lvl < 5:
                def f_skip(v, cur=cur, nxt=nxt):
                    for sl8 in range(8):
                        v.tensor_copy(out=nxt[:, sl8 * 512:(sl8 + 1) * 512],
                                                  in_=cur[:, sl8 * 512:(sl8 + 1) * 512])
                        self.sS.dinc(v)
                self.sec("vector", f_skip)
                continue

            def f_t4(t, c=c, m=m_s1, mv4=m_v4):
                t.wait_ge(tA.h, m)
                t.wait_ge(tV.h, mv4)
                tT.inc(t.matmul(out=pb[3][:, 448:512], lhsT=self.ones1f[:],
                                rhs=self.cdrow[0:1, c * 64:(c + 1) * 64], start=True, stop=True))
            self.sec("tensor", f_t4)
            m_cdrepl = tT.n

            for hq in range(16 if _do_quads else 0):
                qi = quad_idx[0]
                quad_idx[0] += 1
                par = qi % 2
                psA = pb[par]
                egb = eg[par]

                def f_te(t, hq=hq, c=c, psA=psA, qi=qi):
                    if qi >= 2:
                        t.wait_ge(sA.h, qi - 1)
                    li = None
                    for h4 in range(4):
                        h = hq * 4 + h4
                        sel = self.id_bf[0:64, h:h + 1].broadcast_to([64, 128])
                        t.matmul(out=psA[:, h4 * 128:(h4 + 1) * 128], lhsT=sel,
                                 rhs=self.cshi[0:64, c * CH:(c + 1) * CH],
                                 start=True, stop=False)
                        li = t.matmul(out=psA[:, h4 * 128:(h4 + 1) * 128], lhsT=sel,
                                      rhs=self.cslo[0:64, c * CH:(c + 1) * CH],
                                      start=False, stop=True)
                    tT.inc(li)
                self.sec("tensor", f_te)
                m_te = tT.n

                e32 = eg32[par]
                if not hasattr(self, "_m_exp"):
                    self._m_exp = [0, 0]
                    self._m_ty = [0, 0]

                def f_vc(v, psA=psA, e32=e32, m=m_te, mx=self._m_exp[par]):
                    v.wait_ge(tT.h, m)
                    if mx:
                        v.wait_ge(tA.h, mx)
                    v.tensor_copy(out=e32[:, :], in_=psA[:, :])
                    sA.dinc(v)
                self.sec("vector", f_vc)
                m_vc = sA.n

                def f_ge(g, e32=e32, m=m_vc):
                    g.wait_ge(sA.h, m)
                    tG.inc(g.affine_select(e32[:].rearrange("p (h i) -> p h i", h=4),
                                           e32[:].rearrange("p (h i) -> p h i", h=4),
                                           pattern=[[0, 4], [1, 128]],
                                           compare_op=ALU.is_ge, fill=self.fill_val,
                                           base=0, channel_multiplier=-1))
                self.sec("gpsimd", f_ge)
                m_ge = tG.n

                def f_se(sc, hq=hq, e32=e32, egb=egb, m=m_ge, mv=m_v1,
                         mty=self._m_ty[par]):
                    sc.wait_ge(tG.h, m)
                    sc.wait_ge(tV.h, mv)
                    if mty:
                        sc.wait_ge(tT.h, mty)
                    li = None
                    for h4 in range(4):
                        h = hq * 4 + h4
                        li = sc.activation(out=egb[:, h4 * 128:(h4 + 1) * 128],
                                           in_=e32[:, h4 * 128:(h4 + 1) * 128],
                                           func=AF.Exp, bias=negcsT[:, h:h + 1])
                    tA.inc(li)
                self.sec("scalar", f_se)
                self._m_exp[par] = tA.n
                m_se = tA.n

                def f_vm(v, c=c, egb=egb, m=m_se, mg=m_v4):
                    v.wait_ge(tA.h, m)
                    v.wait_ge(tV.h, mg)
                    v.tensor_tensor(
                        out=egb[:].rearrange("p (h i) -> p h i", h=4),
                        in0=egb[:].rearrange("p (h i) -> p h i", h=4),
                        in1=self.g_sb[:, None, c * CH:(c + 1) * CH].broadcast_to([128, 4, CH]),
                        op=ALU.mult)
                    sM.dinc(v)
                self.sec("vector", f_vm)
                m_vm = sM.n
                if self.debug and c == 0 and hq == 0:
                    def f_dbg4(sc, m=m_vm):
                        sc.wait_ge(sM.h, m)
                        dC.inc(sc.dma_start(out=self.d_eg2.ap(), in_=eg[0][:, :]), 16)
                        dC.inc(sc.dma_start(out=self.d_cshi.ap(), in_=self.cshi[:, :]), 16)
                        dC.inc(sc.dma_start(out=self.d_cslo.ap(), in_=self.cslo[:, :]), 16)
                    self.sec("scalar", f_dbg4)
                if self.debug and c == 0 and hq == 0:
                    def f_dbg3(sc, m=m_vm):
                        sc.wait_ge(sM.h, m)
                        dC.inc(sc.dma_start(out=self.d_eg.ap(), in_=eg[0][:, :]), 16)
                    self.sec("scalar", f_dbg3)

                pby = pb[6 + (qi % 2)]

                def f_ty(t, hq=hq, egb=egb, pby=pby, m=m_vm, md=m_dtx, qi=qi):
                    t.wait_ge(sM.h, m)
                    t.wait_ge(tV.h, md)
                    if qi >= 2:
                        t.wait_ge(sY.h, qi - 1)
                    li = None
                    for h4 in range(4):
                        h = hq * 4 + h4
                        li = t.matmul(out=pby[0:64, h4 * 128:(h4 + 1) * 128],
                                      lhsT=dtx[:, h * 64:(h + 1) * 64],
                                      rhs=egb[:, h4 * 128:(h4 + 1) * 128], start=True, stop=True)
                    tT.inc(li)
                self.sec("tensor", f_ty)
                m_ty = tT.n
                self._m_ty[par] = tT.n

                def f_vy(v, hq=hq, pby=pby, m=m_ty, mld=m_ld, myd=m_ystg_dma):
                    v.wait_ge(tT.h, m)
                    v.wait_ge(dS.h, mld)
                    if myd:
                        v.wait_ge(dC.h, myd)
                    for h4 in range(4):
                        h = hq * 4 + h4
                        li = v.scalar_tensor_tensor(
                            out=ystg[0:64, h * 128:(h + 1) * 128],
                            in0=xfmc[0:64, h * 128:(h + 1) * 128],
                            scalar=self.dexp2[:, h:h + 1],
                            in1=pby[0:64, h4 * 128:(h4 + 1) * 128],
                            op0=ALU.mult, op1=ALU.add)
                    sY.inc(li)
                self.sec("vector", f_vy)

                if hq % 4 == 3 and z_next[0] < NZ and _do_z:
                    self.emit_z_unit(z_next[0])
                    z_next[0] += 1

            if _pre_lvl < 6:
                def f_skip(v, cur=cur, nxt=nxt):
                    for sl8 in range(8):
                        v.tensor_copy(out=nxt[:, sl8 * 512:(sl8 + 1) * 512],
                                                  in_=cur[:, sl8 * 512:(sl8 + 1) * 512])
                        self.sS.dinc(v)
                self.sec("vector", f_skip)
                continue

            def f_v5(v, cur=cur, mlc=m_lc_dma):
                if mlc:
                    v.wait_ge(dC.h, mlc)
                v.tensor_copy(out=lcstg[:], in_=cur[:, :])
                tV.dinc(v)
            self.sec("vector", f_v5)
            m_lcst = tV.n

            def f_s2(sc, c=c, m=m_lcst):
                sc.wait_ge(tV.h, m)
                dC.inc(sc.dma_start(out=self.s_lc[c, :, :], in_=lcstg[:]), 16)
            self.sec("scalar", f_s2)
            m_lc_dma = dC.n

            if not _do_states:
                def f_vs0(v, cur=cur, nxt=nxt):
                    sS, tVl = self.sS, self.tV
                    for sl8 in range(8):
                        v.tensor_copy(out=nxt[:, sl8 * 512:(sl8 + 1) * 512],
                                             in_=cur[:, sl8 * 512:(sl8 + 1) * 512])
                        sS.dinc(v)
                self.sec("vector", f_vs0)
                continue
            sS_base = sS.n

            def f_ts(t, c=c, m=m_v4, md=m_dtx, base=sS_base, mb=m_bmt):
                t.wait_ge(tV.h, max(m, md))
                t.wait_ge(dS.h, mb)
                for sl8 in range(8):
                    if base + sl8 - 1 > 0:
                        t.wait_ge(sS.h, base + sl8 - 1)
                    tT.inc(t.matmul(out=pb[4 + sl8 % 2][:, :], lhsT=BmT[:, :],
                                    rhs=dtxd[:, sl8 * 512:(sl8 + 1) * 512], start=True, stop=True))
            self.sec("tensor", f_ts)
            m_ts_base = tT.n - 8

            def f_vs(v, cur=cur, nxt=nxt, mb=m_ts_base, mcd=m_cdrepl, mlc=m_lcst):
                v.wait_ge(tT.h, mcd)
                v.wait_ge(tV.h, mlc)
                for sl8 in range(8):
                    v.wait_ge(tT.h, mb + sl8 + 1)
                    v.tensor_tensor(
                        out=nxt[:, sl8 * 512:(sl8 + 1) * 512].rearrange("p (h q) -> p h q", h=8),
                        in0=cur[:, sl8 * 512:(sl8 + 1) * 512].rearrange("p (h q) -> p h q", h=8),
                        in1=pb[3][:, 448 + sl8 * 8:456 + sl8 * 8][:, :, None].broadcast_to([128, 8, 64]),
                        op=ALU.mult)
                    v.tensor_tensor(out=nxt[:, sl8 * 512:(sl8 + 1) * 512],
                                           in0=nxt[:, sl8 * 512:(sl8 + 1) * 512],
                                           in1=pb[4 + sl8 % 2][:, :], op=ALU.add)
                    sS.dinc(v)
            self.sec("vector", f_vs)

            if not _do_quads:
                continue

            def f_s3(sc, c=c, m=sS.n):
                sc.wait_ge(sS.h, m)
                dC.inc(sc.dma_start(
                    out=bass.AP(self.s_y, c * CH, [[T, 64], [64 * T, 64], [1, CH]]),
                    in_=ystg[:].rearrange("p (h i) -> p h i", h=64)), 16)
            self.sec("scalar", f_s3)
            m_ystg_dma = dC.n

        while z_next[0] < NZ and _do_z:
            self.emit_z_unit(z_next[0])
            z_next[0] += 1
        self.final_carry = self.carry[NCH % 2]
        ar16.pop()
        self._ssd32_open = True

    def stage_ag(self):
        dG, cc, tV = self.dG, self.cc, self.tV
        fc = self.final_carry
        spad = self.ar32.take(64)

        def f_vz(v):
            v.memset(spad[:, :], 0.0)
            v.drain()
            v.tensor_copy(out=spad[0:1, :], in_=self.stab[0:1, NCH * 64:(NCH + 1) * 64])
            tV.dinc(v)
        self.sec("vector", f_vz)
        m_sp = tV.n

        def f_gp(g):
            g.wait_ge(tV.h, m_sp)
            dG.inc(g.dma_start(out=self.cc_in[:, 0:DIN], in_=fc[:, :]), 16)
            dG.inc(g.dma_start(out=self.cc_in[:, DIN:DIN + 64], in_=spad[:, :]), 16)
            g.wait_ge(dG.h, dG.n)
            cc.inc(g.collective_compute(
                "AllGather", ALU.bypass, replica_groups=[list(range(8))],
                ins=[self.cc_in.ap().opt()], outs=[self.cc_out.ap().opt()]), 1)
            g.wait_ge(cc.h, cc.n)
        self.sec("gpsimd", f_gp)

    def stage_pcomb(self):
        ar32 = self.ar32
        if self._ssd32_open:
            self.ar32.pop()
            self._ssd32_open = False
        ar32.push()
        P = ar32.take(DIN)
        aq = ar32.take(DIN)
        t32 = ar32.take(DIN)
        self.P = P
        dS, tT, tV, tA = self.dS, self.tT, self.tV, self.tA
        pb = self.pb
        RL = DIN + 64

        def f_sy(s):
            dS.inc(s.dma_start(out=self.smat[:, :],
                               in_=bass.AP(self.cc_out, DIN, [[128 * RL, 8], [1, 64]])), 16)
        self.sec("sync", f_sy)
        m_sm = dS.n

        def f_t(t):
            t.wait_ge(dS.h, m_sm)
            tT.inc(t.matmul(out=pb[0][0:8, 0:64], lhsT=self.chainMT[:, :], rhs=self.smat[:, :],
                            start=True, stop=True))
        self.sec("tensor", f_t)
        m_mm = tT.n

        def f_sc(sc):
            sc.wait_ge(tT.h, m_mm)
            sc.activation(out=self.coeff0[:, :], in_=pb[0][0:8, 0:64], func=AF.Exp)
            tA.inc(sc.activation(out=self.expcum[0:1, :], in_=self.stab[0:1, 0:NCH * 64],
                                 func=AF.Exp))
        self.sec("scalar", f_sc)
        m_co = tA.n

        def f_v0(v):
            v.wait_ge(tA.h, m_co)
            v.tensor_scalar(out=self.coeff[:, :], in0=self.coeff0[:, :],
                            scalar1=self.indcol[:, 0:1], scalar2=None, op0=ALU.mult)
            v.memset(P[:, :], 0.0)
            tV.dinc(v)
        self.sec("vector", f_v0)
        m_v0 = tV.n

        for qp in range(8):
            def f_sy2(s, qp=qp, mu=tV.n):
                s.wait_ge(tV.h, mu)
                dS.inc(s.dma_start(out=aq[:, :], in_=self.cc_out[qp * 128:(qp + 1) * 128, 0:DIN]), 16)
            self.sec("sync", f_sy2)
            m_aq = dS.n

            def f_t2(t, qp=qp, mv=max(m_v0, tV.n)):
                t.wait_ge(tV.h, mv)
                tT.inc(t.matmul(out=pb[1][:, 0:64],
                                lhsT=self.id_f32[0:8, qp:qp + 1].broadcast_to([8, 128]),
                                rhs=self.coeff[0:8, 0:64], start=True, stop=True))
            self.sec("tensor", f_t2)
            m_rep = tT.n

            def f_v1(v, m=m_aq, mr=m_rep):
                v.wait_ge(dS.h, m)
                v.wait_ge(tT.h, mr)
                v.tensor_tensor(out=t32[:].rearrange("p (h q) -> p h q", h=64),
                                in0=aq[:].rearrange("p (h q) -> p h q", h=64),
                                in1=pb[1][:, 0:64][:, :, None].broadcast_to([128, 64, 64]),
                                op=ALU.mult)
                v.tensor_tensor(out=P[:, :], in0=P[:, :], in1=t32[:, :], op=ALU.add)
                tV.dinc(v)
            self.sec("vector", f_v1)
        if self.debug:
            def f_dbgP(sc, m=tV.n):
                sc.wait_ge(tV.h, m)
                self.dC.inc(sc.dma_start(out=self.d_P.ap(), in_=P[:, :]), 16)
            self.sec("scalar", f_dbgP)
        ar32.pop()
        ar32.push()
        self.P = ar32.take(DIN)

    def stage_gate_wout(self):
        ar32, ar16 = self.ar32, self.ar16
        ar32.push(); ar16.push()
        t32 = ar32.take(DIN)
        prevs = [ar16.take(DIN) for _ in range(4)]
        lcr = [ar16.take(DIN), ar16.take(DIN)]
        yr = [ar16.take(512), ar16.take(512)]
        zr = [ar16.take(512), ar16.take(512)]
        e2 = ar16.take(512)
        tb = ar16.take(512)
        yu = ar16.take(512)
        szt = ar16.take(512)
        sqs = ar16.take(512)
        ygsb = ar16.take(32 * 512)
        rmsrow = ar16.take(512)
        rms32 = ar32.take(1024, rows=1)
        wom = ar16.take(DIN)
        att = [ar16.take(512), ar16.take(512)]
        dS, dC, dG = self.dS, self.dC, self.dG
        tT, tV, tA = self.tT, self.tV, self.tA
        sW = self.sW
        pb = self.pb
        P = self.P

        wo_free = [0]
        for sb in range(2):
            for ci in range(4):
                c = sb * 4 + ci

                def f_sy(s, c=c, ci=ci, mu=tV.n):
                    if mu:
                        s.wait_ge(tV.h, mu)
                    dS.inc(s.dma_start(out=lcr[ci % 2][:, :], in_=self.s_lc[c, :, :]), 16)
                self.sec("sync", f_sy)
                m_lc = dS.n

                def f_t(t, c=c, mprev=tV.n):
                    if mprev:
                        t.wait_ge(tV.h, mprev)
                    tT.inc(t.matmul(out=pb[2][:, 0:64], lhsT=self.ones1f[:],
                                    rhs=self.expcum[0:1, c * 64:(c + 1) * 64], start=True, stop=True))
                self.sec("tensor", f_t)
                m_rep = tT.n

                def f_v(v, ci=ci, m=m_lc, mr=m_rep):
                    v.wait_ge(tT.h, mr)
                    v.tensor_tensor(out=t32[:].rearrange("p (h q) -> p h q", h=64),
                                    in0=P[:].rearrange("p (h q) -> p h q", h=64),
                                    in1=pb[2][:, 0:64][:, :, None].broadcast_to([128, 64, 64]),
                                    op=ALU.mult)
                    v.wait_ge(dS.h, m)
                    v.tensor_tensor(out=prevs[ci][:, :], in0=t32[:, :],
                                           in1=lcr[ci % 2][:, :], op=ALU.add)
                    tV.dinc(v)
                self.sec("vector", f_v)
            m_prevs = tV.n
            if self.debug and sb == 0:
                def f_dbgpr(sc, m=m_prevs):
                    sc.wait_ge(tV.h, m)
                    self.dC.inc(sc.dma_start(out=self.d_prev0.ap(), in_=prevs[0][:, :]), 16)
                self.sec("scalar", f_dbgpr)

            for g in range(32):
                def f_sy2(s, g=g, sb=sb, mu=tV.n if g >= 2 else 0):
                    if mu:
                        s.wait_ge(tV.h, mu)
                    dS.inc(s.dma_start(out=yr[g % 2][:, :],
                                       in_=bass.AP(self.s_y, g * 128 * T + sb * 512,
                                                   [[T, 128], [1, 512]])), 16)
                    dS.inc(s.dma_start(out=zr[g % 2][:, :],
                                       in_=bass.AP(self.s_z, g * 128 * T + sb * 512,
                                                   [[T, 128], [1, 512]])), 16)
                self.sec("sync", f_sy2)
                m_yz = dS.n

                def f_t2(t, g=g, sb=sb, mp=m_prevs, me=tV.n):
                    t.wait_ge(tV.h, mp)
                    if me:
                        t.wait_ge(tV.h, me)
                    for ci in range(4):
                        c = sb * 4 + ci
                        t.matmul(out=pb[3][:, ci * 128:(ci + 1) * 128],
                                 lhsT=prevs[ci][:, g * 128:(g + 1) * 128],
                                 rhs=self.c_sb[:, c * CH:(c + 1) * CH], start=True, stop=True)
                    tT.inc(t.matmul(out=pb[4][:, :],
                                    lhsT=self.sel2t[:, g * 128:(g + 1) * 128],
                                    rhs=self.expcs[0:64, sb * 512:(sb + 1) * 512],
                                    start=True, stop=True))
                self.sec("tensor", f_t2)
                m_t2 = tT.n

                def f_v2(v, g=g, m=m_t2, myz=m_yz):
                    v.wait_ge(tT.h, m)
                    v.tensor_copy(out=e2[:, :], in_=pb[4][:, :])
                    v.drain()
                    v.tensor_tensor(out=tb[:, :], in0=pb[3][:, :], in1=e2[:, :], op=ALU.mult)
                    v.wait_ge(dS.h, myz)
                    v.tensor_tensor(out=yu[:, :], in0=yr[g % 2][:, :], in1=tb[:, :],
                                           op=ALU.add)
                    tV.dinc(v)
                self.sec("vector", f_v2)
                m_v2 = tV.n

                def f_sc2(sc, g=g, m=m_yz, mv=m_v2):
                    sc.wait_ge(dS.h, m)
                    tA.inc(sc.activation(out=szt[:, :], in_=zr[g % 2][:, :], func=AF.Silu))
                self.sec("scalar", f_sc2)
                m_sz = tA.n

                def f_v3(v, g=g, m=m_sz):
                    v.wait_ge(tA.h, m)
                    v.tensor_tensor(out=ygsb[:, g * 512:(g + 1) * 512],
                                           in0=yu[:, :], in1=szt[:, :], op=ALU.mult)
                    tV.dinc(v)
                self.sec("vector", f_v3)
                m_yg = tV.n

                def f_sc3(sc, g=g, m=m_yg):
                    sc.wait_ge(tV.h, m)
                    tA.inc(sc.activation(out=sqs[:, :], in_=ygsb[:, g * 512:(g + 1) * 512],
                                         func=AF.Square))
                self.sec("scalar", f_sc3)
                m_sq = tA.n

                def f_t3(t, g=g, m=m_sq):
                    t.wait_ge(tA.h, m)
                    tT.inc(t.matmul(out=pb[5][0:1, :], lhsT=self.onescolb[:, :], rhs=sqs[:, :],
                                    start=(g == 0), stop=(g == 31)))
                self.sec("tensor", f_t3)
            m_ss = self.tT.n

            def f_v4b(v, m=m_ss):
                v.wait_ge(tT.h, m)
                v.tensor_scalar(out=rms32[0:1, 0:512], in0=pb[5][0:1, :], scalar1=1.0 / DIN,
                                scalar2=EPS, op0=ALU.mult, op1=ALU.add)
                v.drain()
                v.reciprocal(out=rms32[0:1, 512:1024], in_=rms32[0:1, 0:512])
                tV.dinc(v)
            self.sec("vector", f_v4b)
            m_rv = tV.n

            def f_sc4(sc, m=m_rv):
                sc.wait_ge(tV.h, m)
                tA.inc(sc.activation(out=rmsrow[0:1, 0:512], in_=rms32[0:1, 512:1024],
                                     func=AF.Sqrt))
            self.sec("scalar", f_sc4)
            m_rms = tA.n

            for g in range(32):
                def f_t4(t, g=g, m=m_rms, me=tV.n):
                    t.wait_ge(tA.h, m)
                    if g >= 2:
                        t.wait_ge(tV.h, me)
                    tT.inc(t.matmul(out=pb[6 + g % 2][:, :], lhsT=self.normw[0:1, g * 128:(g + 1) * 128],
                                    rhs=rmsrow[0:1, 0:512], start=True, stop=True))
                self.sec("tensor", f_t4)
                m_t4 = tT.n

                def f_v5(v, g=g, m=m_t4):
                    v.wait_ge(tT.h, m)
                    v.tensor_tensor(out=ygsb[:, g * 512:(g + 1) * 512],
                                           in0=ygsb[:, g * 512:(g + 1) * 512],
                                           in1=pb[6 + g % 2][:, :], op=ALU.mult)
                    tV.dinc(v)
                self.sec("vector", f_v5)
            m_yn = tV.n

            for d in range(16):
                def f_gp(g_, d=d, fm=wo_free[0]):
                    if fm:
                        g_.wait_ge(tT.h, fm)
                    self.dG.inc(g_.dma_start(out=wom[:, :], in_=self.i_wout[d, :, :]), 16)
                self.sec("gpsimd", f_gp)
                m_w = dG.n

                def f_t5(t, d=d, m=m_yn, mw=m_w, me=tV.n):
                    t.wait_ge(tV.h, m)
                    t.wait_ge(dG.h, mw)
                    if me:
                        t.wait_ge(tV.h, me)
                    li = None
                    for k in range(32):
                        li = t.matmul(out=pb[0][:, :], lhsT=wom[:, k * 128:(k + 1) * 128],
                                      rhs=ygsb[:, k * 512:(k + 1) * 512],
                                      start=(k == 0), stop=(k == 31))
                    tT.inc(li)
                self.sec("tensor", f_t5)
                wo_free[0] = tT.n
                m_t5 = tT.n

                def f_v6(v, d=d, m=m_t5, md=dC.n):
                    v.wait_ge(tT.h, m)
                    if md:
                        v.wait_ge(dC.h, md)
                    v.tensor_copy(out=att[d % 2][:, :], in_=pb[0][:, :])
                    tV.dinc(v)
                self.sec("vector", f_v6)
                m_att = tV.n

                def f_sc5(sc, d=d, sb=sb, m=m_att):
                    sc.wait_ge(tV.h, m)
                    dC.inc(sc.dma_start(
                        out=bass.AP(self.s_attnT, d * 128 * T + sb * 512, [[T, 128], [1, 512]]),
                        in_=att[d % 2][:, :]), 16)
                self.sec("scalar", f_sc5)
        ar32.pop(); ar16.pop()

    def stage_resid1(self):
        ar32, ar16 = self.ar32, self.ar16
        ar32.push(); ar16.push()
        xm = ar32.take(D)
        xt = ar32.take(D)
        at = ar16.take(D)
        dS, dC, tV = self.dS, self.dC, self.tV
        for tt in range(8):
            def f_sy(s, tt=tt, mu=tV.n):
                if mu:
                    s.wait_ge(tV.h, mu)
                dS.inc(s.dma_start(out=at[:, :], in_=self.s_attnT[:, tt * 128:(tt + 1) * 128],
                                   transpose=True), 16)
                dS.inc(s.dma_start(out=xt[:, :], in_=self.i_x[PRE + tt * 128:PRE + (tt + 1) * 128, :]), 16)
            self.sec("sync", f_sy)
            m_ld = dS.n

            def f_v(v, m=m_ld, md=dC.n):
                v.wait_ge(dS.h, m)
                if md:
                    v.wait_ge(dC.h, md)
                v.tensor_tensor(out=xm[:, :], in0=xt[:, :], in1=at[:, :], op=ALU.add)
                tV.dinc(v)
            self.sec("vector", f_v)
            m_xm = tV.n

            def f_sc(sc, tt=tt, m=m_xm):
                sc.wait_ge(tV.h, m)
                dC.inc(sc.dma_start(out=self.s_xmid[tt * 128:(tt + 1) * 128, :], in_=xm[:, :]), 16)
            self.sec("scalar", f_sc)
        ar32.pop(); ar16.pop()

    def stage_ffn(self):
        ar16 = self.ar16
        dS, dC, dG = self.dS, self.dC, self.dG
        tT, tV, tA, sW = self.tT, self.tV, self.tA, self.sW
        pb = self.pb

        ar16.push()
        wb0 = [ar16.take(1024)[:, :].bitcast(F8) for _ in range(2)]
        wb1 = [ar16.take(1024)[:, :].bitcast(F8) for _ in range(2)]
        gto = ar16.take(T)
        acto = ar16.take(T)
        h2T8 = ar16.take(NKD * T // 2)[:, :].bitcast(F8)
        tV, tA = self.tV, self.tA

        def f_vc8(v):
            for kp in range(NKD // 2):
                v.tensor_copy(
                    out=h2T8[:, kp * 2048:(kp + 1) * 2048].rearrange(
                        "p (blk i c) -> p blk i c", blk=2, i=2, c=512),
                    in_=self.h2T[:, kp * 2048:(kp + 1) * 2048].rearrange(
                        "p (i blk c) -> p blk i c", i=2, blk=2, c=512))
            tV.dinc(v)
        self.sec("vector", f_vc8)
        m_h28 = tV.n
        wg_free = [0, 0]
        for o in range(NFF):
            par = o % 2

            def f_gp(g, o=o, par=par, fm=wg_free[par]):
                if fm:
                    g.wait_ge(tT.h, fm)
                self.dG.inc(g.dma_start(out=wb0[par][:, :], in_=self.i_wg0[o, :, :]), 16)
                self.dG.inc(g.dma_start(out=wb1[par][:, :], in_=self.i_wg1[o, :, :]), 16)
            self.sec("gpsimd", f_gp)
            m_w = dG.n

            def f_t(t, o=o, par=par, mw=m_w, ma=tA.n, mv=tV.n):
                t.wait_ge(dG.h, mw)
                t.wait_ge(tV.h, m_h28)
                if ma:
                    t.wait_ge(tA.h, ma)
                if mv:
                    t.wait_ge(tV.h, mv)
                def pr2(ap, n):
                    return ap.rearrange("p (i c) -> p i c", i=2, c=n)
                for kp in range(NKD // 2):
                    for blk in range(2):
                        t.matmul(out=pb[blk][:, :],
                                 lhsT=pr2(wb0[par][:, kp * 256:(kp + 1) * 256], 128),
                                 rhs=pr2(h2T8[:, (kp * 2 + blk) * 1024:(kp * 2 + blk) * 1024 + 1024], 512),
                                 start=(kp == 0), stop=(kp == NKD // 2 - 1),
                                 perf_mode=PM.DoubleRow)
                li = None
                for kp in range(NKD // 2):
                    for blk in range(2):
                        li = t.matmul(out=pb[2 + blk][:, :],
                                      lhsT=pr2(wb1[par][:, kp * 256:(kp + 1) * 256], 128),
                                      rhs=pr2(h2T8[:, (kp * 2 + blk) * 1024:(kp * 2 + blk) * 1024 + 1024], 512),
                                      start=(kp == 0), stop=(kp == NKD // 2 - 1),
                                      perf_mode=PM.DoubleRow)
                tT.inc(li)
            self.sec("tensor", f_t)
            wg_free[par] = tT.n
            m_mm = tT.n

            def f_sc(sc, o=o, m=m_mm):
                sc.wait_ge(tT.h, m)
                li = None
                for blk in range(2):
                    li = sc.activation(out=gto[:, blk * 512:(blk + 1) * 512], in_=pb[blk][:, :],
                                       func=AF.Gelu_apprx_tanh)
                tA.inc(li)
            self.sec("scalar", f_sc)
            m_gelu = tA.n

            def f_v(v, o=o, m=m_gelu, mm=m_mm, md=dG.n):
                v.wait_ge(tA.h, m)
                v.wait_ge(tT.h, mm)
                if md:
                    v.wait_ge(dG.h, md)
                li = None
                for blk in range(2):
                    li = v.tensor_tensor(out=acto[:, blk * 512:(blk + 1) * 512],
                                         in0=gto[:, blk * 512:(blk + 1) * 512],
                                         in1=pb[2 + blk][:, :], op=ALU.mult)
                tV.inc(li)
            self.sec("vector", f_v)
            m_act = tV.n

            def f_s2(g, o=o, m=m_act):
                g.wait_ge(tV.h, m)
                dG.inc(g.dma_start(out=self.s_act[o * 128:(o + 1) * 128, :], in_=acto[:, :]), 16)
            self.sec("gpsimd", f_s2)
        ar16.pop()
        self._end_j1()

        ar16.push()
        actsb = ar16.take(NFF * 512 // 2)[:, :].bitcast(F8)
        wlb = [ar16.take(DFF // 2)[:, :].bitcast(F8) for _ in range(2)]
        fst = [ar16.take(512), ar16.take(512)]
        wl_free = [0, 0]
        for blk in range(2):
            def f_sy(s, blk=blk, mu=tV.n):
                if mu:
                    s.wait_ge(tV.h, mu)
                dS.inc(s.dma_start(
                    out=actsb.rearrange("p (f i) -> p f i", f=NFF),
                    in_=bass.AP(self.s_act, blk * 512, [[T, 128], [128 * T, NFF], [1, 512]])), 16)
            self.sec("sync", f_sy)
            m_act2 = dS.n

            for d in range(16):
                par = d % 2

                def f_gp(g, d=d, par=par, fm=wl_free[par]):
                    if fm:
                        g.wait_ge(tT.h, fm)
                    self.dG.inc(g.dma_start(out=wlb[par][:, :], in_=self.i_wlin[d, :, :]), 16)
                self.sec("gpsimd", f_gp)
                m_w = dG.n

                def f_t(t, d=d, par=par, mw=m_w, ma=m_act2, me=tV.n):
                    t.wait_ge(dG.h, mw)
                    t.wait_ge(dS.h, ma)
                    if me:
                        t.wait_ge(tV.h, me)
                    li = None
                    for kp in range(NFF // 2):
                        li = t.matmul(
                            out=pb[4 + d % 2][:, :],
                            lhsT=wlb[par][:, kp * 256:(kp + 1) * 256].rearrange(
                                "p (i c) -> p i c", i=2, c=128),
                            rhs=actsb[:, kp * 1024:(kp + 1) * 1024].rearrange(
                                "p (i c) -> p i c", i=2, c=512),
                            start=(kp == 0), stop=(kp == NFF // 2 - 1),
                            perf_mode=PM.DoubleRow)
                    tT.inc(li)
                self.sec("tensor", f_t)
                wl_free[par] = tT.n
                m_mm = tT.n

                def f_v(v, d=d, m=m_mm, md=dC.n):
                    v.wait_ge(tT.h, m)
                    if md:
                        v.wait_ge(dC.h, md)
                    v.tensor_copy(out=fst[d % 2][:, :], in_=pb[4 + d % 2][:, :])
                    tV.dinc(v)
                self.sec("vector", f_v)
                m_f = tV.n

                def f_sc(sc, d=d, blk=blk, m=m_f):
                    sc.wait_ge(tV.h, m)
                    dC.inc(sc.dma_start(
                        out=bass.AP(self.s_ffnT, d * 128 * T + blk * 512, [[T, 128], [1, 512]]),
                        in_=fst[d % 2][:, :]), 16)
                self.sec("scalar", f_sc)
        ar16.pop()

    def _end_j1(self):
        self.barrier()
        self.ar16.pop()
        self._h2t_popped = True

    def stage_final(self):
        ar32, ar16 = self.ar32, self.ar16
        ar32.push(); ar16.push()
        xm = ar32.take(D)
        ot = ar32.take(D)
        ft = ar16.take(D)
        dS, dC, tV = self.dS, self.dC, self.tV
        for tt in range(8):
            def f_sy(s, tt=tt, mu=tV.n):
                if mu:
                    s.wait_ge(tV.h, mu)
                dS.inc(s.dma_start(out=ft[:, :], in_=self.s_ffnT[:, tt * 128:(tt + 1) * 128],
                                   transpose=True), 16)
                dS.inc(s.dma_start(out=xm[:, :], in_=self.s_xmid[tt * 128:(tt + 1) * 128, :]), 16)
            self.sec("sync", f_sy)
            m_ld = dS.n

            def f_v(v, m=m_ld, md=dC.n):
                v.wait_ge(dS.h, m)
                if md:
                    v.wait_ge(dC.h, md)
                v.tensor_tensor(out=ot[:, :], in0=xm[:, :], in1=ft[:, :], op=ALU.add)
                tV.dinc(v)
            self.sec("vector", f_v)
            m_o = tV.n

            def f_sc(sc, tt=tt, m=m_o):
                sc.wait_ge(tV.h, m)
                dC.inc(sc.dma_start(out=self.o_out[tt * 128:(tt + 1) * 128, :], in_=ot[:, :]), 16)
            self.sec("scalar", f_sc)
        ar32.pop(); ar16.pop()

    def build(self):
        import os
        stop = os.environ.get("K_STOP", "")
        nc = self.nc
        with nc.Block() as block:
            self.block = block

            def _stop(name):
                if stop == name:
                    self.barrier()
                    raise _StopBuild()
            self.stage_consts()
            self.ar16.push()
            self.hlnT = self.ar16.take(NKD * TL)
            self.wmbuf = [self.ar16.take(2048), self.ar16.take(2048)]
            try:
                self.stage_ln(self.i_x, self.s_hln, self.hlnT, TL, with_pre=True)
                self.m_hlnT = self.dS.n
                _stop("ln")
                self.stage_dt()
                _stop("dt")
                self.stage_xbc()
                self.barrier()
                _stop("xbc")
                self.stage_ssd1()
                self.ar16.pop()
                self.barrier()
                _stop("ssd1")
                self.stage_ag()
                self.barrier()
                _stop("ag")
                self.stage_pcomb()
                _stop("pcomb")
                self.stage_gate_wout()
                self.barrier()
                _stop("gate")
                self.stage_resid1()
                self.barrier()
                _stop("resid")
                self.ar16.push()
                self.h2T = self.ar16.take(NKD * T)
                self.stage_ln(self.s_xmid, self.s_h2, self.h2T, T, with_pre=False)
                self.barrier()
                _stop("ln2")
                self.stage_ffn()
                self.barrier()
                _stop("ffn")
                self.stage_final()
                self.barrier()
            except _StopBuild:
                pass
        return nc


def _sel2t():
    s = np.zeros((64, 32 * 128), np.float32)
    for g in range(32):
        s[2 * g, g * 128:g * 128 + 64] = 1.0
        s[2 * g + 1, g * 128 + 64:(g + 1) * 128] = 1.0
    return s.astype(BF)


def make_inputs(ins, core):
    b, q = core // 4, core % 4
    x = np.asarray(ins["L0_attn_hidden_states"])
    t0 = q * T
    main = x[b, t0:t0 + T]
    pre = np.zeros((PRE, D), np.float32) if q == 0 else x[b, t0 - PRE:t0]
    xs = np.ascontiguousarray(np.concatenate([pre, main], axis=0), dtype=np.float32)

    w_in = np.asarray(ins["L0_attn_W_in"]).astype(np.float32)
    w_in_p = np.zeros((D, 67 * 128), np.float32)
    w_in_p[:, :DPROJ] = w_in
    w_in_t = np.ascontiguousarray(
        w_in_p.reshape(NKD, 128, 67, 128).transpose(2, 1, 0, 3).reshape(67, 128, 2048))
    w_out_t = np.ascontiguousarray(
        np.asarray(ins["L0_attn_W_out"]).astype(np.float32)
        .reshape(32, 128, 16, 128).transpose(2, 1, 0, 3).reshape(16, 128, 4096))
    wg = np.asarray(ins["L0_ffn_w_gating"]).astype(np.float32)
    wg0_t = np.ascontiguousarray(
        wg[0].reshape(16, 128, 64, 128).transpose(2, 1, 0, 3).reshape(64, 128, 2048))
    wg1_t = np.ascontiguousarray(
        wg[1].reshape(16, 128, 64, 128).transpose(2, 1, 0, 3).reshape(64, 128, 2048))
    wl_t = np.ascontiguousarray(
        np.asarray(ins["L0_ffn_w_linear"]).astype(np.float32)
        .reshape(64, 128, 16, 128).transpose(2, 1, 0, 3).reshape(16, 128, 8192))

    ind = np.zeros(8, np.float32)
    chainM = np.zeros((8, 8), np.float32)
    for qp in range(4):
        src = b * 4 + qp
        if qp < q:
            ind[src] = 1.0
            for k in range(qp + 1, q):
                chainM[src, b * 4 + k] = 1.0
    chainMT = np.ascontiguousarray(chainM.T)

    return {
        "x_slice": xs,
        "w_in_t": w_in_t,
        "w_out_t": w_out_t,
        "w_g0_t": wg0_t,
        "w_g1_t": wg1_t,
        "w_lin_t": wl_t,
        "conv_w": np.ascontiguousarray(np.asarray(ins["L0_attn_conv_w"]).astype(np.float32)),
        "conv_b": np.ascontiguousarray(np.asarray(ins["L0_attn_conv_b"]).astype(np.float32)),
        "A_vec": np.ascontiguousarray(np.asarray(ins["L0_attn_A"]).astype(np.float32)),
        "D_vec": np.ascontiguousarray(np.asarray(ins["L0_attn_D"]).astype(np.float32)),
        "dt_bias": np.ascontiguousarray(np.asarray(ins["L0_attn_dt_bias"]).astype(np.float32)),
        "norm_w": np.ascontiguousarray(np.asarray(ins["L0_attn_norm_w"]).astype(np.float32)),
        "id_bf": np.eye(128, dtype=BF),
        "id_f32": np.eye(128, dtype=np.float32),
        "sel2": np.stack([np.concatenate([np.ones(64), np.zeros(64)]),
                          np.concatenate([np.zeros(64), np.ones(64)])]).astype(BF),
        "sel2t": _sel2t(),
        "chainMT": chainMT,
        "ind_mask": ind,
    }


def kernel(**inputs):
    b = Builder(debug=DEBUG)
    nc = b.build()
    in_maps = [make_inputs(inputs, c) for c in range(8)]
    res = run_bass_kernel_spmd(nc, in_maps, core_ids=list(range(8)))
    out = np.zeros((B, S, D), np.float32)
    for c in range(8):
        bb, q = c // 4, c % 4
        out[bb, q * T:(q + 1) * T] = res.results[c]["out"]
    return out
